# version 07
# baseline (speedup 1.0000x reference)
import math
import numpy as np
import ml_dtypes
import concourse.bass as bass
import concourse.mybir as mybir
from concourse.bass_utils import run_bass_kernel_spmd

F32 = mybir.dt.float32
BF16 = mybir.dt.bfloat16
U8 = mybir.dt.uint8
AF = mybir.ActivationFunctionType
ALU = mybir.AluOpType
AX = mybir.AxisListType

NCORES = 8
S = 8192
D = 2048
H = 16
FF = 5632
T = S // NCORES
QT = T // 128
NB = S // 128
HC = FF // 128
DC = D // 128
CSC = 1.0 / math.sqrt(128.0)
EPS_LN = 1e-5
EPS_BL = 1e-8
NW = float(FF * D)

K_OFF = 0
K_SZ = H * 128 * T
V_OFF = K_OFF + K_SZ
V_SZ = H * 128 * QT * 128
N_OFF = V_OFF + V_SZ
PAYLOAD = N_OFF + 64



class _Track:
    _NOREC = {"wait_ge", "wait_op", "sem_inc", "maybe_wait_ge"}

    def __init__(self, eng):
        object.__setattr__(self, "_e", eng)
        object.__setattr__(self, "last", None)

    def __getattr__(self, name):
        f = getattr(object.__getattribute__(self, "_e"), name)
        if callable(f) and not name.startswith("_"):
            def wrap(*a, __f=f, __n=name, **k):
                r = __f(*a, **k)
                if __n not in _Track._NOREC and hasattr(r, "then_inc"):
                    object.__setattr__(self, "last", r)
                return r
            return wrap
        return f


def build_core_program(debug=False):
    nc = bass.Bass("TRN2", target_bir_lowering=False, detect_race_conditions=False)

    x_in = nc.dram_tensor("x", [T, D], F32, kind="ExternalInput").ap()
    wq_r = nc.dram_tensor("wq_r", [DC, 128, DC, 128], F32, kind="ExternalInput").ap()
    wk_r = nc.dram_tensor("wk_r", [DC, 128, DC, 128], F32, kind="ExternalInput").ap()
    wv_r = nc.dram_tensor("wv_r", [DC, 128, DC, 128], F32, kind="ExternalInput").ap()
    wo_r = nc.dram_tensor("wo_r", [4, H, 128, 512], F32, kind="ExternalInput").ap()
    bq_c = nc.dram_tensor("bq_c", [128, DC], F32, kind="ExternalInput").ap()
    bk_c = nc.dram_tensor("bk_c", [128, DC], F32, kind="ExternalInput").ap()
    bv_c = nc.dram_tensor("bv_c", [128, DC], F32, kind="ExternalInput").ap()
    bo_r = nc.dram_tensor("bo_r", [1, D], F32, kind="ExternalInput").ap()
    w1_r = nc.dram_tensor("w1_r", [HC, 128, DC, 128], F32, kind="ExternalInput").ap()
    w2_r = nc.dram_tensor("w2_r", [HC, 128, DC, 128], F32, kind="ExternalInput").ap()
    w3t = nc.dram_tensor("w3t", [FF, D], F32, kind="ExternalInput").ap()
    cos_hi = nc.dram_tensor("cos_hi", [128, T], F32, kind="ExternalInput").ap()
    cos_lo = nc.dram_tensor("cos_lo", [128, T], F32, kind="ExternalInput").ap()
    sin_hi = nc.dram_tensor("sin_hi", [128, T], F32, kind="ExternalInput").ap()
    sin_lo = nc.dram_tensor("sin_lo", [128, T], F32, kind="ExternalInput").ap()
    y_out = nc.dram_tensor("y", [T, D], F32, kind="ExternalOutput").ap()
    if debug:
        dbg_lnx = nc.dram_tensor("dbg_lnx", [128, DC * T], BF16, kind="ExternalOutput").ap()
        dbg_q = nc.dram_tensor("dbg_q", [128, H * T], BF16, kind="ExternalOutput").ap()
        dbg_pay = nc.dram_tensor("dbg_pay", [1, PAYLOAD], BF16, kind="ExternalOutput").ap()
        dbg_att = nc.dram_tensor("dbg_att", [128, H * T], BF16, kind="ExternalOutput").ap()
        dbg_x1 = nc.dram_tensor("dbg_x1", [T, D], F32, kind="ExternalOutput").ap()
        dbg_lnx0 = nc.dram_tensor("dbg_lnx0", [128, DC * T], BF16, kind="ExternalOutput").ap()
        dbg_qlin0 = nc.dram_tensor("dbg_qlin0", [128, T], BF16, kind="ExternalOutput").ap()
        dbg_mneg = nc.dram_tensor("dbg_mneg", [128, H], F32, kind="ExternalOutput").ap()
        dbg_qe = nc.dram_tensor("dbg_qe", [128, H * T], BF16, kind="ExternalOutput").ap()
        dbg_tab = nc.dram_tensor("dbg_tab", [128, 2 * T], F32, kind="ExternalOutput").ap()
        dbg_qsw = nc.dram_tensor("dbg_qsw", [128, T], BF16, kind="ExternalOutput").ap()
        dbg_scr = nc.dram_tensor("dbg_scr", [128, 2 * T], F32, kind="ExternalOutput").ap()
        dbg_tau = nc.dram_tensor("dbg_tau", [128, 8], F32, kind="ExternalOutput").ap()
        dbg_grow = nc.dram_tensor("dbg_grow", [1, 512], F32, kind="ExternalOutput").ap()
        dbg_gacc = nc.dram_tensor("dbg_gacc", [128, 8], F32, kind="ExternalOutput").ap()
        dbg_jd = nc.dram_tensor("dbg_jd", [128, 132], F32, kind="ExternalOutput").ap()
        dbg_q1 = nc.dram_tensor("dbg_q1", [128, D], BF16, kind="ExternalOutput").ap()
        dbg_q3 = nc.dram_tensor("dbg_q3", [128, 512], BF16, kind="ExternalOutput").ap()
        dbg_rows = nc.dram_tensor("dbg_rows", [1, 4 * 16 + 128], F32, kind="ExternalOutput").ap()
        dbg_sums = nc.dram_tensor("dbg_sums", [1, T], F32, kind="ExternalOutput").ap()
        dbg_rbc = nc.dram_tensor("dbg_rbc", [128, T], F32, kind="ExternalOutput").ap()

    cc_in = nc.dram_tensor("cc_in", [1, PAYLOAD], BF16, kind="Internal").ap()
    cc_out = nc.dram_tensor("cc_out", [NCORES, PAYLOAD], BF16, kind="Internal").ap()
    q1_scr = nc.dram_tensor("q1_scr", [HC, 128, D], BF16, kind="Internal").ap()
    q2_scr = nc.dram_tensor("q2_scr", [HC, 128, D], BF16, kind="Internal").ap()
    q3_scr = nc.dram_tensor("q3_scr", [4, HC, 128, 512], BF16, kind="Internal").ap()
    x1_scr = nc.dram_tensor("x1_scr", [T, D], F32, kind="Internal").ap()

    ident = nc.inline_tensor(np.eye(128, dtype=ml_dtypes.bfloat16), name="ident_bf").ap()

    def A(name, shape, dt):
        return nc.alloc_sbuf_tensor(name, shape, dt).ap()
    id_sb = A("id_sb", [128, 128], BF16)
    ones_cb = A("ones_cb", [128, 1], BF16)
    ones_rf = A("ones_rf", [1, 128], F32)
    bqc_sb = A("bqc_sb", [128, DC], F32)
    bkc_sb = A("bkc_sb", [128, DC], F32)
    bvc_sb = A("bvc_sb", [128, DC], F32)
    q_rot = A("q_rot", [128, H * T], BF16)
    pt_sb = A("pt_sb", [128, 2 * T], BF16)
    recip_bc = A("recip_bc", [128, T], F32)
    sums_f = A("sums_f", [1, T], F32)
    ln_row = A("ln_row", [1, T], F32)
    qmax_sb = A("qmax_sb", [1, H], F32)
    kmax_sb = A("kmax_sb", [1, H], F32)
    kml_bf = A("kml_bf", [1, 64], BF16)
    kg_bf = A("kg_bf", [1, NCORES * H], BF16)
    kgmax_sb = A("kgmax_sb", [1, H], F32)
    row_a = A("row_a", [1, 16], F32)
    row_b = A("row_b", [1, 16], F32)
    mrow = A("mrow", [1, H], F32)
    mneg_bc = A("mneg_bc", [128, H], F32)
    g_acc = A("g_acc", [128, 8], F32)
    g_part = [A("g_part0", [128, 1], F32), A("g_part1", [128, 1], F32)]
    jd_sb = A("jd_sb", [128, 132], F32) if debug else None
    grow = A("grow", [1, 1024], F32)
    tau4 = A("tau4", [1, 4], F32)
    tau_row = A("tau_row", [1, 8], F32)
    tau_bc = A("tau_bc", [128, 8], F32)
    eps_pm = A("eps_pm", [1, 2], F32)
    eps_ln_bc = A("eps_ln_bc", [128, 1], F32)
    st1 = A("st1", [128, 1], F32)
    st2 = A("st2", [128, 1], F32)
    st3 = A("st3", [128, 1], F32)
    st_m = A("st_m", [128, 1], F32)
    st_rs = A("st_rs", [128, 1], F32)
    st_vpe = A("st_vpe", [128, 1], F32)
    st_sq = A("st_sq", [128, 1], F32)

    AR1 = nc.alloc_sbuf_tensor("AR1", [128, 49152], U8)
    AR2 = nc.alloc_sbuf_tensor("AR2", [128, 32768], U8)
    AR3 = nc.alloc_sbuf_tensor("AR3", [128, 65536], U8)

    def vf(ar, off, cols):
        return ar.ap().bitcast(F32)[:, off // 4: off // 4 + cols]

    def vb(ar, off, cols):
        return ar.ap().bitcast(BF16)[:, off // 2: off // 2 + cols]

    cos_t = vf(AR1, 0, T)
    sinm_t = vf(AR1, 4096, T)
    qlin = [vb(AR1, 8192 + i * 2048, T) for i in range(2)]
    qsw = [vb(AR1, 12288 + i * 2048, T) for i in range(2)]
    vlin = [vb(AR1, 16384 + i * 2048, T) for i in range(2)]
    vst_h = [vb(AR1, 20480 + i * 2048, T) for i in range(2)]
    krot_st = [vb(AR1, 24576 + i * 2048, T) for i in range(2)]
    sq_st = [vb(AR1, 28672 + i * 2048, T) for i in range(2)]
    lo_t = vf(AR1, 32768, T)
    lo_t2 = vf(AR1, 36864, T)
    att_fm = [vb(AR1, h * 2048, T) for h in range(H)]
    lnx2_st = [vb(AR1, 32768 + i * 4096, D) for i in range(2)]
    bo_bc = vf(AR1, 40960, D)
    y3_st = [vb(AR1, qc * 4096, D) for qc in range(4)]
    x1_t = [vf(AR1, 16384 + i * 8192, D) for i in range(2)]
    scr5 = vf(AR1, 32768, D)

    lnx_fm = [vb(AR2, i * 2048, T) for i in range(DC)]
    qz_in = [vf(AR2, i * 8192, D) for i in range(2)]
    qz_out = [vb(AR2, 16384 + i * 4096, D) for i in range(2)]
    qz_a = vb(AR2, 24576, D)

    x_t = [vf(AR3, i * 8192, D) for i in range(2)]
    wqkv_b = [vb(AR3, 16384 + i * 4096, D) for i in range(2)]
    scr = vf(AR3, 24576, D)
    lnx_st = [vb(AR3, 32768 + i * 4096, D) for i in range(2)]
    k_sb = [vb(AR3, i * 16384, S) for i in range(2)]
    v_sb = [vb(AR3, 32768 + i * 16384, S) for i in range(2)]
    oy_st = [vb(AR3, qt * 4096, D) for qt in range(QT)]
    x4_t = [vf(AR3, 32768 + i * 8192, D) for i in range(2)]
    wo_b = [vb(AR3, 49152 + i * 1024, 512) for i in range(2)]
    scr4 = vf(AR3, 51200, D)
    hid_sb = [vb(AR3, hc * 1024, 512) for hc in range(HC)]
    q1_b = [vb(AR3, 45056 + i * 4096, D) for i in range(2)]
    q2_b = [vb(AR3, 53248 + i * 4096, D) for i in range(2)]
    q3_b = [vb(AR3, 61440 + i * 1024, 512) for i in range(2)]
    sig_sb = [vb(AR3, 63488 + i * 1024, 512) for i in range(2)]

    AR2_bf3 = AR2.ap().bitcast(BF16).rearrange("p (c f) -> p c f", f=1024)
    AR1_bf3 = AR1.ap().bitcast(BF16).rearrange("p (c f) -> p c f", f=1024)

    psA = nc.alloc_psum_tensor("PS_A", [128, 2048], F32).ap()
    psB = nc.alloc_psum_tensor("PS_B", [128, 1024], F32).ap()
    psC = nc.alloc_psum_tensor("PS_C", [128, 1024], F32).ap()
    psB_bf = psB.bitcast(BF16)

    single = ["s_small", "s_ln0", "s_trp", "s_trev", "s_mm1p", "s_ev1",
              "s_rope", "s_vt", "s_vtev", "s_npay", "s_sq", "s_nrm", "s_nmax",
              "s_cc", "s_kg", "s_bp", "s_bound", "s_bo", "s_mnb", "s_taub",
              "s_sc", "s_exp", "s_pv", "s_rec", "s_bc", "s_hd", "s_gamv",
              "s_gam", "s_tau", "s_qz", "s_omm", "s_oev", "s_ln1", "s_tr2p",
              "s_tr2ev", "s_m12", "s_sig", "s_hid", "s_m3", "s_y3", "s_outv",
              "s_lnv", "s_lns"]
    pairs = ["s_x", "s_wdma", "s_qswA", "s_qswB", "s_kpay", "s_vpay", "s_k",
             "s_v", "s_gw", "s_qzw", "s_qzo", "s_x4", "s_wo", "s_x1s",
             "s_q1", "s_q2", "s_q3", "s_x1l", "s_out"]
    SM = {n: nc.alloc_semaphore(n) for n in single}
    for n in pairs:
        SM[n] = [nc.alloc_semaphore(n + "0"), nc.alloc_semaphore(n + "1")]

    def dwait(eng, pair, t):
        eng.wait_ge(pair[t % 2], 16 * (t // 2 + 1))

    with nc.Block() as block:

        @block.sync
        def _(sy):
            sy = _Track(sy)
            sy.dma_start(out=id_sb, in_=ident[:, :]).then_inc(SM["s_small"], 16)
            sy.dma_start(out=bqc_sb, in_=bq_c[:, :]).then_inc(SM["s_small"], 16)
            sy.dma_start(out=bkc_sb, in_=bk_c[:, :]).then_inc(SM["s_small"], 16)
            sy.dma_start(out=bvc_sb, in_=bv_c[:, :]).then_inc(SM["s_small"], 16)
            sy.dma_start(out=cos_t, in_=cos_hi[:, :]).then_inc(SM["s_small"], 16)
            sy.dma_start(out=sinm_t, in_=sin_hi[:, :]).then_inc(SM["s_small"], 16)
            sy.dma_start(out=lo_t, in_=cos_lo[:, :]).then_inc(SM["s_small"], 16)
            sy.dma_start(out=lo_t2, in_=sin_lo[:, :]).then_inc(SM["s_small"], 16)
            for t in range(QT):
                if t >= 2:
                    sy.wait_ge(SM["s_ln0"], t - 1)
                sy.dma_start(out=x_t[t % 2],
                             in_=x_in[t * 128:(t + 1) * 128, :]).then_inc(SM["s_x"][t % 2], 16)
            if debug:
                sy.wait_ge(SM["s_trev"], 32)
                for i in range(DC):
                    sy.dma_start(out=dbg_lnx0[:, i * T:(i + 1) * T], in_=lnx_fm[i]).then_inc(SM["s_npay"], 16)
                sy.wait_ge(SM["s_ev1"], 1)
                sy.dma_start(out=dbg_qlin0, in_=qlin[0]).then_inc(SM["s_npay"], 16)
            for idx in range(2 * DC):
                pp = idx % 2
                sy.wait_ge(SM["s_ev1"], idx + 1)
                if idx >= 2:
                    sy.wait_ge(SM["s_rope"], idx - 1)
                sy.dma_start(out=qsw[pp][0:64, :], in_=qlin[pp][64:128, :]).then_inc(SM["s_qswA"][pp], 16)
                sy.dma_start(out=qsw[pp][64:128, :], in_=qlin[pp][0:64, :]).then_inc(SM["s_qswB"][pp], 16)
                if idx >= DC:
                    h = idx - DC
                    sy.wait_ge(SM["s_rope"], DC + h + 1)
                    sy.dma_start(
                        out=cc_in[0:1, K_OFF + h * 128 * T:K_OFF + (h + 1) * 128 * T]
                            .rearrange("one (p t) -> (one p) t", p=128),
                        in_=krot_st[h % 2]).then_inc(SM["s_kpay"][h % 2], 16)
            sy.wait_ge(SM["s_rope"], DC + H)
            sy.dma_start(
                out=cc_in[0:1, K_OFF + (H - 1) * 128 * T:K_OFF + H * 128 * T]
                    .rearrange("one (p t) -> (one p) t", p=128),
                in_=krot_st[(H - 1) % 2]).then_inc(SM["s_kpay"][(H - 1) % 2], 16)
            for dc in range(H):
                sy.wait_ge(SM["s_vtev"], 2 * (dc + 1))
                sy.dma_start(
                    out=cc_in[0:1, V_OFF + dc * 128 * T:V_OFF + (dc + 1) * 128 * T]
                        .rearrange("one (p f) -> (one p) f", p=128),
                    in_=vst_h[dc % 2]).then_inc(SM["s_vpay"][dc % 2], 16)
            sy.wait_ge(SM["s_nmax"], 2 * H + 1)
            sy.dma_start(out=cc_in[0:1, N_OFF:N_OFF + 64], in_=kml_bf).then_inc(SM["s_npay"], 16)
            sy.dma_start(out=bo_bc[0:1, :], in_=bo_r[:, :]).then_inc(SM["s_bo"], 16)
            p, step = 1, 1
            while p < 128:
                sy.wait_ge(SM["s_bo"], 16 * step)
                sy.dma_start(out=bo_bc[p:2 * p, :], in_=bo_bc[0:p, :]).then_inc(SM["s_bo"], 16)
                p *= 2
                step += 1
            sy.wait_ge(SM["s_cc"], 1)
            if debug:
                sy.dma_start(out=dbg_qe, in_=q_rot).then_inc(SM["s_npay"], 16)
                sy.dma_start(out=dbg_tab[:, 0:T], in_=cos_t).then_inc(SM["s_npay"], 16)
                sy.dma_start(out=dbg_tab[:, T:2 * T], in_=sinm_t).then_inc(SM["s_npay"], 16)
                sy.dma_start(out=dbg_qsw, in_=qsw[0]).then_inc(SM["s_npay"], 16)
                sy.dma_start(out=dbg_scr, in_=scr[:, 0:2 * T]).then_inc(SM["s_npay"], 16)
            sy.dma_start(out=kg_bf.rearrange("one (c h) -> one c h", c=NCORES),
                         in_=cc_out[:, N_OFF:N_OFF + H].unsqueeze(0)
                         ).then_inc(SM["s_kg"], 16)
            gci = [0]

            def sy_gamma_loads(n):
                for _ in range(n):
                    ci = gci[0]
                    if ci >= 3 * HC:
                        return
                    wi, ch = ci // HC, ci % HC
                    if ci >= 2:
                        sy.wait_ge(SM["s_gamv"], ci - 1)
                    src_w = [w1_r, w2_r, w3t][wi]
                    if wi < 2:
                        sy.dma_start(out=qz_in[ci % 2].rearrange("p (i c) -> p i c", i=DC),
                                     in_=src_w[ch, :, :, :]).then_inc(SM["s_gw"][ci % 2], 16)
                    else:
                        sy.dma_start(out=qz_in[ci % 2],
                                     in_=src_w[ch * 128:(ch + 1) * 128, :]).then_inc(SM["s_gw"][ci % 2], 16)
                    gci[0] += 1

            for h in range(H):
                if h >= 2:
                    sy.wait_ge(SM["s_hd"], h - 1)
                sy.dma_start(
                    out=k_sb[h % 2].rearrange("p (c t) -> p c t", c=NCORES),
                    in_=cc_out[:, K_OFF + h * 128 * T:K_OFF + (h + 1) * 128 * T]
                        .rearrange("c (p t) -> p c t", p=128)).then_inc(SM["s_k"][h % 2], 16)
                sy.dma_start(
                    out=v_sb[h % 2].rearrange("p (c lb f) -> p c lb f", c=NCORES, lb=QT),
                    in_=cc_out[:, V_OFF + h * 128 * QT * 128:V_OFF + (h + 1) * 128 * QT * 128]
                        .rearrange("c (p lb f) -> p c lb f", p=128, lb=QT)).then_inc(SM["s_v"][h % 2], 16)
                sy_gamma_loads(17)
            sy_gamma_loads(3 * HC)
            sy.wait_ge(SM["s_hd"], H)
            for t in range(QT):
                if t >= 2:
                    dwait(sy, SM["s_x1s"], t - 2)
                sy.dma_start(out=x4_t[t % 2],
                             in_=x_in[t * 128:(t + 1) * 128, :]).then_inc(SM["s_x4"][t % 2], 16)
                if t >= 1:
                    ts = t - 1
                    sy.wait_ge(SM["s_ln1"], ts + 1)
                    sy.dma_start(out=x1_scr[ts * 128:(ts + 1) * 128, :],
                                 in_=x4_t[ts % 2]).then_inc(SM["s_x1s"][ts % 2], 16)
            sy.wait_ge(SM["s_ln1"], QT)
            sy.dma_start(out=x1_scr[(QT - 1) * 128:QT * 128, :],
                         in_=x4_t[(QT - 1) % 2]).then_inc(SM["s_x1s"][(QT - 1) % 2], 16)
            sy.wait_ge(SM["s_ln1"], QT)

            def x1_reload(t):
                if t >= 2:
                    dwait(sy, SM["s_out"], t - 2)
                sy.dma_start(out=x1_t[t % 2],
                             in_=x1_scr[t * 128:(t + 1) * 128, :]).then_inc(SM["s_x1l"][t % 2], 16)

            sy.wait_ge(SM["s_x1s"][0], 16 * (QT // 2))
            sy.wait_ge(SM["s_x1s"][1], 16 * (QT // 2))
            sy.wait_ge(SM["s_omm"], 4 * H)
            x1_reload(0)
            x1_reload(1)
            for half in range(2):
                for hc in range(HC):
                    idx = half * HC + hc
                    if idx >= 2:
                        sy.wait_ge(SM["s_m12"], 2 * (idx - 1))
                    sy.dma_start(out=q1_b[idx % 2], in_=q1_scr[hc, :, :]).then_inc(SM["s_q1"][idx % 2], 16)
                    sy.dma_start(out=q2_b[idx % 2], in_=q2_scr[hc, :, :]).then_inc(SM["s_q2"][idx % 2], 16)
                if half == 1:
                    x1_reload(4)
                    x1_reload(5)
                for dq in range(4):
                    for hc in range(HC):
                        idx3 = (half * 4 + dq) * HC + hc
                        if idx3 >= 2:
                            sy.wait_ge(SM["s_m3"], idx3 - 1)
                        sy.dma_start(out=q3_b[idx3 % 2],
                                     in_=q3_scr[dq, hc, :, :]).then_inc(SM["s_q3"][idx3 % 2], 16)
                if half == 0:
                    x1_reload(2)
                    x1_reload(3)
            x1_reload(6)
            x1_reload(7)
            sy.wait_ge(SM["s_out"][0], 16 * (QT // 2))
            sy.wait_ge(SM["s_out"][1], 16 * (QT // 2))
            if debug:
                sy.wait_ge(SM["s_npay"], 16 * (1 + DC + 1 + 3 + H + 1 + DC + 1 + 7 + 5 + 4))

        @block.gpsimd
        def _(g):
            g = _Track(g)
            for wi, wr in enumerate([wq_r, wk_r, wv_r]):
                for dc in range(DC):
                    idx = wi * DC + dc
                    if idx >= 2:
                        g.wait_ge(SM["s_mm1p"], idx - 1)
                    g.dma_start(out=wqkv_b[idx % 2].rearrange("p (i c) -> p i c", i=DC),
                                in_=wr[dc, :, :, :]).then_inc(SM["s_wdma"][idx % 2], 16)
            g.wait_ge(SM["s_kpay"][0], 16 * (H // 2))
            g.wait_ge(SM["s_kpay"][1], 16 * (H // 2))
            g.wait_ge(SM["s_vpay"][0], 16 * (H // 2))
            g.wait_ge(SM["s_vpay"][1], 16 * (H // 2))
            g.wait_ge(SM["s_npay"], 16)
            g.collective_compute(
                "AllGather", ALU.bypass, replica_groups=[list(range(NCORES))],
                ins=[cc_in[:, :]], outs=[cc_out[:, :]],
            ).then_inc(SM["s_cc"], 1)
            g.wait_ge(SM["s_bound"], 1)
            g.dma_start(out=mneg_bc[0:1, :], in_=mrow).then_inc(SM["s_mnb"], 16)
            p, stp = 1, 1
            while p < 128:
                g.wait_ge(SM["s_mnb"], 16 * stp)
                g.dma_start(out=mneg_bc[p:2 * p, :], in_=mneg_bc[0:p, :]).then_inc(SM["s_mnb"], 16)
                p *= 2
                stp += 1
            g.wait_ge(SM["s_gamv"], 3 * HC + 1)
            for wq_i in range(8):
                g.dma_start(out=grow[0:1, wq_i * 128:(wq_i + 1) * 128],
                            in_=g_acc[:, wq_i:wq_i + 1]).then_inc(SM["s_gam"], 16)
            g.wait_ge(SM["s_tau"], 1)
            g.dma_start(out=tau_bc[0:1, :], in_=tau_row).then_inc(SM["s_taub"], 16)
            p, stp = 1, 1
            while p < 128:
                g.wait_ge(SM["s_taub"], 16 * stp)
                g.dma_start(out=tau_bc[p:2 * p, :], in_=tau_bc[0:p, :]).then_inc(SM["s_taub"], 16)
                p *= 2
                stp += 1
            for wi, (src, dst) in enumerate([(w1_r, q1_scr), (w2_r, q2_scr), (w3t, q3_scr)]):
                for ch in range(HC):
                    idx = wi * HC + ch
                    if idx >= 2:
                        g.wait_ge(SM["s_qz"], idx - 1)
                    if wi < 2:
                        g.dma_start(out=qz_in[idx % 2].rearrange("p (i c) -> p i c", i=DC),
                                    in_=src[ch, :, :, :]).then_inc(SM["s_qzw"][idx % 2], 16)
                    else:
                        g.dma_start(out=qz_in[idx % 2],
                                    in_=src[ch * 128:(ch + 1) * 128, :]).then_inc(SM["s_qzw"][idx % 2], 16)
                    g.wait_ge(SM["s_qz"], idx + 1)
                    if wi < 2:
                        g.dma_start(out=dst[ch, :, :], in_=qz_out[idx % 2]).then_inc(SM["s_qzo"][idx % 2], 16)
                    else:
                        g.dma_start(out=dst.rearrange("dq hc p c -> p dq hc c")[:, :, ch, :],
                                    in_=qz_out[idx % 2].rearrange("p (dq c) -> p dq c", dq=4)
                                    ).then_inc(SM["s_qzo"][idx % 2], 16)
            for dq in range(4):
                for h in range(H):
                    idx = dq * H + h
                    if idx >= 2:
                        g.wait_ge(SM["s_omm"], idx - 1)
                    g.dma_start(out=wo_b[idx % 2], in_=wo_r[dq, h, :, :]).then_inc(SM["s_wo"][idx % 2], 16)

        @block.tensor
        def _(t):
            t = _Track(t)
            t.wait_ge(SM["s_small"], 16 * 8)
            for tile in range(QT):
                t.wait_ge(SM["s_ln0"], tile + 1)
                for grp in range(4):
                    gi = tile * 4 + grp
                    if gi >= 2:
                        t.wait_ge(SM["s_trev"], gi - 1)
                    for j in range(4):
                        dcc = grp * 4 + j
                        t.transpose(psB_bf[:, (gi % 2) * 1024 + j * 128:(gi % 2) * 1024 + j * 128 + 128],
                                    lnx_st[tile % 2][:, dcc * 128:(dcc + 1) * 128],
                                    id_sb)
                    t.last.then_inc(SM["s_trp"], 1)
            sqidx = 0
            first_vt = True
            for wi in range(3):
                for dc in range(DC):
                    idx = wi * DC + dc
                    dwait(t, SM["s_wdma"], idx)
                    if idx == 0:
                        t.wait_ge(SM["s_trev"], 32)
                    if idx >= 2:
                        t.wait_ge(SM["s_ev1"], idx - 1)
                    pp = idx % 2
                    for i in range(DC):
                        for n2 in range(2):
                            t.matmul(psA[:, pp * 1024 + n2 * 512: pp * 1024 + (n2 + 1) * 512],
                                     wqkv_b[pp][:, i * 128:(i + 1) * 128],
                                     lnx_fm[i][:, n2 * 512:(n2 + 1) * 512],
                                     start=(i == 0), stop=(i == DC - 1))
                    t.last.then_inc(SM["s_mm1p"], 1)
                    if wi == 2:
                        t.wait_ge(SM["s_ev1"], idx + 1)
                        if first_vt:
                            t.wait_ge(SM["s_trev"], 32)
                            first_vt = False
                        for grp in range(2):
                            vgi = dc * 2 + grp
                            if vgi >= 2:
                                t.wait_ge(SM["s_vtev"], vgi - 1)
                            for j in range(4):
                                tc = grp * 4 + j
                                t.transpose(psB_bf[:, (vgi % 2) * 1024 + j * 128:(vgi % 2) * 1024 + j * 128 + 128],
                                            vlin[dc % 2][:, tc * 128:(tc + 1) * 128],
                                            id_sb)
                            t.last.then_inc(SM["s_vt"], 1)
                    else:
                        t.wait_ge(SM["s_sq"], sqidx + 1)
                        if sqidx >= 1:
                            t.wait_ge(SM["s_nmax"], sqidx)
                        for n2 in range(2):
                            t.matmul(psC[0:1, n2 * 512:(n2 + 1) * 512],
                                     ones_cb, sq_st[sqidx % 2][:, n2 * 512:(n2 + 1) * 512],
                                     start=True, stop=True)
                        t.last.then_inc(SM["s_nrm"], 1)
                        sqidx += 1
            for h in range(H):
                dwait(t, SM["s_k"], h)
                dwait(t, SM["s_v"], h)
                if h == 0:
                    t.wait_ge(SM["s_nmax"], 2 * H)
                for b in range(NB):
                    idx = h * NB + b
                    pp = idx % 2
                    if idx >= 2:
                        t.wait_ge(SM["s_exp"], idx - 1)
                    for n2 in range(2):
                        t.matmul(psA[:, pp * 1024 + n2 * 512: pp * 1024 + (n2 + 1) * 512],
                                 k_sb[h % 2][:, b * 128:(b + 1) * 128],
                                 q_rot[:, h * T + n2 * 512: h * T + (n2 + 1) * 512],
                                 start=True, stop=True)
                    t.last.then_inc(SM["s_sc"], 1)
                    t.wait_ge(SM["s_exp"], idx + 1)
                    if b == 0 and h > 0:
                        t.wait_ge(SM["s_hd"], h)
                    for n2 in range(2):
                        t.matmul(psB[:, n2 * 512:(n2 + 1) * 512],
                                 v_sb[h % 2][:, b * 128:(b + 1) * 128],
                                 pt_sb[:, pp * T + n2 * 512: pp * T + (n2 + 1) * 512],
                                 start=(b == 0), stop=(b == NB - 1))
                    for n2 in range(2):
                        t.matmul(psC[0:1, n2 * 512:(n2 + 1) * 512],
                                 ones_cb,
                                 pt_sb[:, pp * T + n2 * 512: pp * T + (n2 + 1) * 512],
                                 start=(b == 0), stop=(b == NB - 1))
                    t.last.then_inc(SM["s_pv"], 1)
                t.wait_ge(SM["s_rec"], h + 1)
                for n2 in range(2):
                    t.matmul(psC[:, n2 * 512:(n2 + 1) * 512],
                             ones_rf, sums_f[0:1, n2 * 512:(n2 + 1) * 512],
                             start=True, stop=True)
                t.last.then_inc(SM["s_bc"], 1)
            ps_o = [psA[:, i * 512:(i + 1) * 512] for i in range(4)] + \
                   [psB[:, i * 512:(i + 1) * 512] for i in range(2)] + \
                   [psC[:, i * 512:(i + 1) * 512] for i in range(2)]
            t.wait_ge(SM["s_hd"], H)
            t.wait_ge(SM["s_exp"], H * NB)
            for dq in range(4):
                for h in range(H):
                    idx = dq * H + h
                    dwait(t, SM["s_wo"], idx)
                    if h == 0 and dq > 0:
                        t.wait_ge(SM["s_oev"], dq * QT)
                    for qt in range(QT):
                        t.matmul(ps_o[qt],
                                 att_fm[h][:, qt * 128:(qt + 1) * 128],
                                 wo_b[idx % 2],
                                 start=(h == 0), stop=(h == H - 1))
                    t.last.then_inc(SM["s_omm"], 1)
            t.wait_ge(SM["s_oev"], 4 * QT)
            for tile in range(QT):
                t.wait_ge(SM["s_ln1"], tile + 1)
                for grp in range(4):
                    gi = tile * 4 + grp
                    if gi >= 2:
                        t.wait_ge(SM["s_tr2ev"], gi - 1)
                    for j in range(4):
                        dcc = grp * 4 + j
                        t.transpose(psB_bf[:, (gi % 2) * 1024 + j * 128:(gi % 2) * 1024 + j * 128 + 128],
                                    lnx2_st[tile % 2][:, dcc * 128:(dcc + 1) * 128],
                                    id_sb)
                    t.last.then_inc(SM["s_tr2p"], 1)
            t.wait_ge(SM["s_tr2ev"], 32)
            for half in range(2):
                for hc in range(HC):
                    idx = half * HC + hc
                    dwait(t, SM["s_q1"], idx)
                    dwait(t, SM["s_q2"], idx)
                    if idx >= 2:
                        t.wait_ge(SM["s_sig"], idx - 1)
                        t.wait_ge(SM["s_hid"], idx - 1)
                    for i in range(DC):
                        t.matmul(psB[:, (idx % 2) * 512:((idx % 2) + 1) * 512],
                                 q1_b[idx % 2][:, i * 128:(i + 1) * 128],
                                 lnx_fm[i][:, half * 512:(half + 1) * 512],
                                 start=(i == 0), stop=(i == DC - 1))
                    for i in range(DC):
                        t.matmul(psC[:, (idx % 2) * 512:((idx % 2) + 1) * 512],
                                 q2_b[idx % 2][:, i * 128:(i + 1) * 128],
                                 lnx_fm[i][:, half * 512:(half + 1) * 512],
                                 start=(i == 0), stop=(i == DC - 1))
                    t.last.then_inc(SM["s_m12"], 2)
                for dq in range(4):
                    for hc in range(HC):
                        idx3 = (half * 4 + dq) * HC + hc
                        dwait(t, SM["s_q3"], idx3)
                        t.wait_ge(SM["s_hid"], half * HC + hc + 1)
                        if hc == 0 and (half * 4 + dq) > 0:
                            t.wait_ge(SM["s_y3"], half * 4 + dq)
                        for qc in range(4):
                            t.matmul(psA[:, qc * 512:(qc + 1) * 512],
                                     hid_sb[hc][:, qc * 128:(qc + 1) * 128],
                                     q3_b[idx3 % 2],
                                     start=(hc == 0), stop=(hc == HC - 1))
                        t.last.then_inc(SM["s_m3"], 1)

        def ln_service(sc):
            inv_n = 1.0 / D
            sc.activation(st3, st1, AF.Square, scale=inv_n)
            sc.activation(st_vpe, st3, AF.Identity, scale=-1.0,
                          bias=eps_ln_bc[:, 0:1])
            sc.activation(st_m, st1, AF.Identity, scale=-inv_n)
            sc.activation(st_sq, st2, AF.Ln, scale=inv_n, bias=st_vpe)
            sc.activation(st_rs, st_sq, AF.Exp, scale=-0.5)
            sc.last.then_inc(SM["s_lns"], 1)

        @block.scalar
        def _(sc):
            sc = _Track(sc)
            for t in range(QT):
                sc.wait_ge(SM["s_lnv"], t + 1)
                ln_service(sc)
            sqidx = 0
            for wi in range(3):
                for dc in range(DC):
                    idx = wi * DC + dc
                    sc.wait_ge(SM["s_mm1p"], idx + 1)
                    pp = idx % 2
                    src_ps = psA[:, pp * 1024: pp * 1024 + 1024]
                    if wi == 0:
                        if idx >= 2:
                            dwait(sc, SM["s_qswA"], idx - 2)
                            dwait(sc, SM["s_qswB"], idx - 2)
                            sc.wait_ge(SM["s_rope"], idx - 1)
                        sc.activation(qlin[pp], src_ps, AF.Identity, bias=bqc_sb[:, dc:dc + 1])
                    elif wi == 1:
                        if idx >= 2:
                            dwait(sc, SM["s_qswA"], idx - 2)
                            dwait(sc, SM["s_qswB"], idx - 2)
                            sc.wait_ge(SM["s_rope"], idx - 1)
                        sc.activation(qlin[pp], src_ps, AF.Identity, bias=bkc_sb[:, dc:dc + 1])
                    else:
                        if dc >= 2:
                            sc.wait_ge(SM["s_vt"], 2 * (dc - 1))
                        sc.activation(vlin[dc % 2], src_ps, AF.Identity, bias=bvc_sb[:, dc:dc + 1])
                    sc.last.then_inc(SM["s_ev1"], 1)
                    if wi == 0:
                        sc.wait_ge(SM["s_rope"], dc + 1)
                        if sqidx >= 2:
                            sc.wait_ge(SM["s_nrm"], sqidx - 1)
                        sc.activation(sq_st[sqidx % 2], q_rot[:, dc * T:(dc + 1) * T], AF.Square)
                        sc.last.then_inc(SM["s_sq"], 1)
                        sqidx += 1
                    elif wi == 1:
                        sc.wait_ge(SM["s_rope"], DC + dc + 1)
                        if sqidx >= 2:
                            sc.wait_ge(SM["s_nrm"], sqidx - 1)
                        sc.activation(sq_st[sqidx % 2], krot_st[dc % 2], AF.Square)
                        sc.last.then_inc(SM["s_sq"], 1)
                        sqidx += 1
            sc.wait_ge(SM["s_bp"], 1)
            sc.activation(row_b, row_a, AF.Ln)
            sc.activation(row_b, row_b, AF.Exp, scale=0.5)
            sc.last.then_inc(SM["s_bp"], 1)
            sc.wait_ge(SM["s_mnb"], 16 * 8)
            for h in range(H):
                if h == 8:
                    sc.wait_ge(SM["s_gam"], 16 * 8)
                    for wq_i in range(4):
                        sc.activation(ln_row[0:1, 0:256],
                                      grow[0:1, wq_i * 256:(wq_i + 1) * 256],
                                      AF.Identity, accum_out=tau4[0:1, wq_i:wq_i + 1])
                    sc.activation(tau_row[0:1, 0:4], tau4[0:1, 0:4], AF.Identity,
                                  scale=0.5 / NW, bias=eps_pm[0:1, 0:1])
                    sc.activation(tau_row[0:1, 4:8], tau4[0:1, 0:4], AF.Identity,
                                  scale=-0.5 / NW, bias=eps_pm[0:1, 1:2])
                    sc.last.then_inc(SM["s_tau"], 1)
                for b in range(NB):
                    idx = h * NB + b
                    pp = idx % 2
                    sc.wait_ge(SM["s_sc"], idx + 1)
                    if idx >= 2:
                        sc.wait_ge(SM["s_pv"], idx - 1)
                    sc.activation(pt_sb[:, pp * T:(pp + 1) * T],
                                  psA[:, pp * 1024: pp * 1024 + 1024],
                                  AF.Exp, bias=mneg_bc[:, h:h + 1], scale=CSC)
                    sc.last.then_inc(SM["s_exp"], 1)
                sc.wait_ge(SM["s_pv"], NB * (h + 1))
                sc.activation(ln_row, psC[0:1, 0:1024], AF.Ln)
                sc.activation(sums_f, ln_row, AF.Exp, scale=-1.0)
                sc.last.then_inc(SM["s_rec"], 1)

            for t in range(QT):
                for k in range(2):
                    sc.wait_ge(SM["s_lnv"], QT + 2 * t + k + 1)
                    ln_service(sc)
            if debug:
                sc.wait_ge(SM["s_trev"], 32)
                for i in range(DC):
                    sc.dma_start(out=dbg_lnx[:, i * T:(i + 1) * T], in_=lnx_fm[i]).then_inc(SM["s_npay"], 16)
                sc.wait_ge(SM["s_rope"], DC)
                sc.dma_start(out=dbg_q, in_=q_rot).then_inc(SM["s_npay"], 16)
                sc.wait_ge(SM["s_cc"], 1)
                sc.dma_start(out=dbg_pay[:, K_OFF:K_OFF + K_SZ].rearrange("one (a b) -> (one a) b", a=2048),
                             in_=cc_in[:, K_OFF:K_OFF + K_SZ].rearrange("one (a b) -> (one a) b", a=2048)).then_inc(SM["s_npay"], 16)
                sc.dma_start(out=dbg_pay[:, V_OFF:V_OFF + V_SZ].rearrange("one (a b) -> (one a) b", a=2048),
                             in_=cc_in[:, V_OFF:V_OFF + V_SZ].rearrange("one (a b) -> (one a) b", a=2048)).then_inc(SM["s_npay"], 16)
                sc.dma_start(out=dbg_pay[:, N_OFF:N_OFF + 64],
                             in_=cc_in[:, N_OFF:N_OFF + 64]).then_inc(SM["s_npay"], 16)
                sc.wait_ge(SM["s_hd"], H)
                sc.dma_start(out=dbg_mneg, in_=mneg_bc).then_inc(SM["s_npay"], 16)
                sc.dma_start(out=dbg_rows[:, 0:16], in_=qmax_sb).then_inc(SM["s_npay"], 16)
                sc.dma_start(out=dbg_rows[:, 16:32], in_=kmax_sb).then_inc(SM["s_npay"], 16)
                sc.dma_start(out=dbg_rows[:, 32:48], in_=kgmax_sb).then_inc(SM["s_npay"], 16)
                sc.dma_start(out=dbg_rows[:, 48:64], in_=mrow).then_inc(SM["s_npay"], 16)

                sc.dma_start(out=dbg_sums, in_=sums_f).then_inc(SM["s_npay"], 16)
                sc.wait_ge(SM["s_qzo"][0], 16 * (3 * HC // 2))
                sc.wait_ge(SM["s_qzo"][1], 16 * (3 * HC // 2))
                sc.dma_start(out=dbg_tau, in_=tau_bc).then_inc(SM["s_npay"], 16)
                sc.dma_start(out=dbg_grow, in_=grow[0:1, 0:512]).then_inc(SM["s_npay"], 16)
                sc.dma_start(out=dbg_gacc, in_=g_acc[:, :]).then_inc(SM["s_npay"], 16)
                sc.dma_start(out=dbg_jd, in_=jd_sb[:, :]).then_inc(SM["s_npay"], 16)
                sc.dma_start(out=dbg_q1, in_=q1_scr[0, :, :]).then_inc(SM["s_npay"], 16)
                sc.dma_start(out=dbg_q3, in_=q3_scr[0, 0, :, :]).then_inc(SM["s_npay"], 16)
                sc.dma_start(out=dbg_rbc, in_=recip_bc).then_inc(SM["s_npay"], 16)
                for hh in range(H):
                    sc.dma_start(out=dbg_att[:, hh * T:(hh + 1) * T], in_=att_fm[hh]).then_inc(SM["s_npay"], 16)
                sc.wait_ge(SM["s_x1s"][0], 16 * (QT // 2))
                sc.wait_ge(SM["s_x1s"][1], 16 * (QT // 2))
                sc.dma_start(out=dbg_x1[:, :], in_=x1_scr[:, :]).then_inc(SM["s_npay"], 16)
            for half in range(2):
                for hc in range(HC):
                    idx = half * HC + hc
                    sc.wait_ge(SM["s_m12"], 2 * (idx + 1))
                    if idx >= 2:
                        sc.wait_ge(SM["s_hid"], idx - 1)
                    sc.activation(sig_sb[idx % 2],
                                  psB[:, (idx % 2) * 512:((idx % 2) + 1) * 512],
                                  AF.Sigmoid)
                    sc.last.then_inc(SM["s_sig"], 1)
                for qc in range(4):
                    t = half * 4 + qc
                    sc.wait_ge(SM["s_lnv"], 3 * QT + t + 1)
                    ln_service(sc)
                    sc.wait_ge(SM["s_outv"], t + 1)
                    sc.dma_start(out=y_out[t * 128:(t + 1) * 128, :],
                                 in_=x1_t[t % 2]).then_inc(SM["s_out"][t % 2], 16)

        @block.vector
        def _(v):
            v = _Track(v)
            lnv_count = [0]

            def ln_stats(xin, n_free, extra_rs_scale=None, dump=None):
                if dump is None:
                    dump = scr
                v.scalar_tensor_tensor(dump, xin, 1.0, xin, ALU.bypass, ALU.mult,
                                       accum_out=st2)
                v.tensor_reduce(st1, xin, AX.X, ALU.add)
                lnv_count[0] += 1
                v.last.then_inc(SM["s_lnv"], 1)
                v.wait_ge(SM["s_lns"], lnv_count[0])
                if extra_rs_scale is not None:
                    v.tensor_scalar(st_rs, st_rs, extra_rs_scale, None, ALU.mult)

            v.wait_ge(SM["s_small"], 16 * 8)
            v.tensor_tensor(cos_t, cos_t, lo_t, ALU.add)
            v.tensor_tensor(sinm_t, sinm_t, lo_t2, ALU.add)
            v.memset(ones_cb, 1.0)
            v.memset(ones_rf, 1.0)
            v.memset(kml_bf, 0.0)
            v.memset(eps_pm[0:1, 0:1], 0.5 * EPS_BL)
            v.memset(eps_pm[0:1, 1:2], -0.5 * EPS_BL)
            v.memset(eps_ln_bc, EPS_LN)
            for t in range(QT):
                dwait(v, SM["s_x"], t)
                ln_stats(x_t[t % 2], D)
                v.tensor_scalar(lnx_st[t % 2], x_t[t % 2], st_m, st_rs, ALU.add, ALU.mult)
                v.last.then_inc(SM["s_ln0"], 1)
                for grp in range(4):
                    gi = t * 4 + grp
                    v.wait_ge(SM["s_trp"], gi + 1)
                    out_ap = AR2_bf3[:, grp * 4:grp * 4 + 4, t * 128:t * 128 + 128]
                    v.tensor_copy(out_ap,
                                  psB_bf[:, (gi % 2) * 1024:(gi % 2) * 1024 + 512]
                                  .rearrange("p (c f) -> p c f", f=128))
                    v.last.then_inc(SM["s_trev"], 1)
            for wi in range(2):
                for dc in range(DC):
                    idx = wi * DC + dc
                    pp = idx % 2
                    dwait(v, SM["s_qswA"], idx)
                    dwait(v, SM["s_qswB"], idx)
                    if wi == 1 and dc >= 2:
                        dwait(v, SM["s_kpay"], dc - 2)
                        v.wait_ge(SM["s_sq"], DC + dc - 1)
                    dst = q_rot[:, dc * T:(dc + 1) * T] if wi == 0 else krot_st[dc % 2]
                    v.tensor_tensor(scr[:, 0:T], qlin[pp], cos_t, ALU.mult)
                    v.tensor_tensor(scr[:, T:2 * T], qsw[pp], sinm_t, ALU.mult)
                    v.tensor_tensor(dst, scr[:, 0:T], scr[:, T:2 * T], ALU.add)
                    v.last.then_inc(SM["s_rope"], 1)
                    v.wait_ge(SM["s_nrm"], idx + 1)
                    dstmax = qmax_sb[0:1, dc:dc + 1] if wi == 0 else kmax_sb[0:1, dc:dc + 1]
                    v.tensor_reduce(dstmax, psC[0:1, 0:1024], AX.X, ALU.max)
                    v.last.then_inc(SM["s_nmax"], 1)
            v.tensor_copy(kml_bf[0:1, 0:H], kmax_sb)
            v.last.then_inc(SM["s_nmax"], 1)
            for vgi in range(32):
                dc, grp = vgi // 2, vgi % 2
                v.wait_ge(SM["s_vt"], vgi + 1)
                if dc >= 2 and grp == 0:
                    dwait(v, SM["s_vpay"], dc - 2)
                v.tensor_copy(vst_h[dc % 2][:, grp * 512:(grp + 1) * 512],
                              psB_bf[:, (vgi % 2) * 1024:(vgi % 2) * 1024 + 512])
                v.last.then_inc(SM["s_vtev"], 1)
            v.wait_ge(SM["s_kg"], 16)
            v.tensor_copy(kgmax_sb, kg_bf[0:1, 0:H])
            for cc_i in range(1, NCORES):
                v.tensor_max(kgmax_sb, kgmax_sb,
                             kg_bf[0:1, cc_i * H:(cc_i + 1) * H])
            v.tensor_tensor(row_a, qmax_sb, kgmax_sb, ALU.mult)
            v.last.then_inc(SM["s_bp"], 1)
            v.wait_ge(SM["s_bp"], 2)
            v.tensor_scalar(mrow, row_b, -CSC, 70.0, ALU.mult, ALU.add)
            v.tensor_scalar_min(mrow, mrow, 0.0)
            v.last.then_inc(SM["s_bound"], 1)
            v.memset(g_acc, 0.0)
            job_pos = [0]
            NJOBS = 3 * HC + 3 * HC

            def emit_jobs(nmax):
                for _ in range(nmax):
                    j = job_pos[0]
                    if j >= NJOBS:
                        return
                    def gadd(pj):
                        wj = pj // HC
                        jjp = 2 * wj + (pj % 2)
                        v.tensor_tensor(g_acc[:, jjp:jjp + 1],
                                        g_acc[:, jjp:jjp + 1],
                                        g_part[pj % 2], ALU.add)
                        if pj == 3 * HC - 1:
                            v.last.then_inc(SM["s_gamv"], 1)
                        if debug:
                            v.tensor_copy(jd_sb[:, pj:pj + 1], g_part[pj % 2])

                    if j < 3 * HC:
                        ci = j
                        dwait(v, SM["s_gw"], ci)
                        v.tensor_reduce(g_part[ci % 2], qz_in[ci % 2], AX.X,
                                        ALU.add, apply_absolute_value=True)
                        v.last.then_inc(SM["s_gamv"], 1)
                        if ci >= 1:
                            gadd(ci - 1)
                    else:
                        idx = j - 3 * HC
                        wi = idx // HC
                        if idx == 0:
                            gadd(3 * HC - 1)
                            v.wait_ge(SM["s_taub"], 16 * 8)
                        dwait(v, SM["s_qzw"], idx)
                        if idx >= 2:
                            dwait(v, SM["s_qzo"], idx - 2)
                        win = qz_in[idx % 2]
                        v.tensor_scalar(qz_a, win, tau_bc[:, wi:wi + 1], 2.0,
                                        ALU.is_ge, ALU.mult)
                        v.tensor_scalar(qz_out[idx % 2], win, tau_bc[:, 4 + wi:4 + wi + 1],
                                        -2.0, ALU.is_le, ALU.mult)
                        v.tensor_tensor(qz_out[idx % 2], qz_out[idx % 2], qz_a, ALU.add)
                        v.last.then_inc(SM["s_qz"], 1)
                    job_pos[0] += 1

            for h in range(H):
                v.wait_ge(SM["s_bc"], h + 1)
                v.tensor_copy(recip_bc, psC)
                v.tensor_tensor(att_fm[h], psB, recip_bc, ALU.mult)
                v.last.then_inc(SM["s_hd"], 1)
                emit_jobs(18)
            emit_jobs(NJOBS)
            for dq in range(4):
                v.wait_ge(SM["s_omm"], dq * H + H)
                v.wait_ge(SM["s_bo"], 16 * 8)
                for qt in range(QT):
                    psrc = ([psA[:, i * 512:(i + 1) * 512] for i in range(4)] +
                            [psB[:, i * 512:(i + 1) * 512] for i in range(2)] +
                            [psC[:, i * 512:(i + 1) * 512] for i in range(2)])[qt]
                    v.tensor_tensor(oy_st[qt][:, dq * 512:(dq + 1) * 512], psrc,
                                    bo_bc[:, dq * 512:(dq + 1) * 512], ALU.add)
                    v.last.then_inc(SM["s_oev"], 1)
            v.wait_ge(SM["s_qzo"][0], 16 * (3 * HC // 2))
            v.wait_ge(SM["s_qzo"][1], 16 * (3 * HC // 2))
            for t in range(QT):
                v.wait_ge(SM["s_oev"], 4 * QT)
                dwait(v, SM["s_x4"], t)
                ln_stats(oy_st[t], D, dump=scr4)
                v.tensor_scalar(scr4, oy_st[t], st_m, st_rs, ALU.add, ALU.mult)
                v.tensor_tensor(x4_t[t % 2], x4_t[t % 2], scr4, ALU.add)
                ln_stats(x4_t[t % 2], D, extra_rs_scale=0.5, dump=scr4)
                v.tensor_scalar(lnx2_st[t % 2], x4_t[t % 2], st_m, st_rs, ALU.add, ALU.mult)
                v.last.then_inc(SM["s_ln1"], 1)
                for grp in range(4):
                    gi = t * 4 + grp
                    v.wait_ge(SM["s_tr2p"], gi + 1)
                    out_ap = AR2_bf3[:, grp * 4:grp * 4 + 4, t * 128:t * 128 + 128]
                    v.tensor_copy(out_ap,
                                  psB_bf[:, (gi % 2) * 1024:(gi % 2) * 1024 + 512]
                                  .rearrange("p (c f) -> p c f", f=128))
                    v.last.then_inc(SM["s_tr2ev"], 1)
            v.wait_ge(SM["s_x1s"][0], 16 * (QT // 2))
            v.wait_ge(SM["s_x1s"][1], 16 * (QT // 2))
            for half in range(2):
                for hc in range(HC):
                    idx = half * HC + hc
                    v.wait_ge(SM["s_sig"], idx + 1)
                    v.tensor_tensor(hid_sb[hc], sig_sb[idx % 2],
                                    psC[:, (idx % 2) * 512:((idx % 2) + 1) * 512],
                                    ALU.mult)
                    v.last.then_inc(SM["s_hid"], 1)
                for dq in range(4):
                    v.wait_ge(SM["s_m3"], (half * 4 + dq) * HC + HC)
                    for qc in range(4):
                        v.tensor_copy(y3_st[qc][:, dq * 512:(dq + 1) * 512],
                                      psA[:, qc * 512:(qc + 1) * 512])
                    v.last.then_inc(SM["s_y3"], 1)
                for qc in range(4):
                    t = half * 4 + qc
                    dwait(v, SM["s_x1l"], t)
                    ln_stats(y3_st[qc], D, dump=scr5)
                    v.tensor_scalar(scr5, y3_st[qc], st_m, st_rs, ALU.add, ALU.mult)
                    v.tensor_tensor(x1_t[t % 2], x1_t[t % 2], scr5, ALU.add)
                    v.last.then_inc(SM["s_outv"], 1)

    return nc


_TABLES = None


def _make_tables():
    global _TABLES
    if _TABLES is not None:
        return _TABLES
    HDh = 64
    pos = np.arange(S, dtype=np.float64)
    invf = 1.0 / (10000.0 ** (np.arange(0, 128, 2, dtype=np.float64) / 128.0))
    sinusoid = pos[:, None] * invf[None, :]
    emb = np.concatenate([np.sin(sinusoid), np.cos(sinusoid)], axis=-1)
    cos_e = np.cos(emb).T
    sin_e = np.sin(emb).T
    sinm = np.concatenate([-sin_e[0:HDh], sin_e[HDh:]], axis=0)

    def split(xx):
        hi = xx.astype(np.float32)
        lo = (xx - hi.astype(np.float64)).astype(np.float32)
        return np.ascontiguousarray(hi), np.ascontiguousarray(lo)

    _TABLES = (*split(cos_e), *split(sinm))
    return _TABLES


def prepare_in_maps(L0_attn_x, L0_attn_Wq, L0_attn_bq, L0_attn_Wk, L0_attn_bk,
                    L0_attn_Wv, L0_attn_bv, L0_attn_Wo, L0_attn_bo,
                    L0_ffn_w1, L0_ffn_w2, L0_ffn_w3):
    f = np.ascontiguousarray
    x = np.asarray(L0_attn_x, dtype=np.float32).reshape(S, D)
    Wq = np.asarray(L0_attn_Wq, np.float32)
    Wk = np.asarray(L0_attn_Wk, np.float32)
    Wv = np.asarray(L0_attn_Wv, np.float32)
    Wo = np.asarray(L0_attn_Wo, np.float32)
    w1 = np.asarray(L0_ffn_w1, np.float32)
    w2 = np.asarray(L0_ffn_w2, np.float32)
    w3 = np.asarray(L0_ffn_w3, np.float32)

    wq_r = f(Wq.reshape(DC, 128, DC, 128).transpose(0, 3, 2, 1))
    wk_r = f(Wk.reshape(DC, 128, DC, 128).transpose(0, 3, 2, 1))
    wv_r = f(Wv.reshape(DC, 128, DC, 128).transpose(0, 3, 2, 1))
    wo_r = f(Wo.reshape(4, 512, H, 128).transpose(0, 2, 3, 1))
    bq_c = f(np.asarray(L0_attn_bq, np.float32).reshape(DC, 128).T)
    bk_c = f(np.asarray(L0_attn_bk, np.float32).reshape(DC, 128).T)
    bv_c = f(np.asarray(L0_attn_bv, np.float32).reshape(DC, 128).T)
    bo_r = f(np.asarray(L0_attn_bo, np.float32).reshape(1, D))
    w1_r = f(w1.reshape(HC, 128, DC, 128).transpose(0, 3, 2, 1))
    w2_r = f(w2.reshape(HC, 128, DC, 128).transpose(0, 3, 2, 1))
    w3t = f(w3.T)

    cos_hi_a, cos_lo_a, sinm_hi_a, sinm_lo_a = _make_tables()

    nc = build_core_program()
    in_maps = []
    for c in range(NCORES):
        sl = slice(c * T, (c + 1) * T)
        in_maps.append({
            "x": f(x[sl]),
            "wq_r": wq_r, "wk_r": wk_r, "wv_r": wv_r, "wo_r": wo_r,
            "bq_c": bq_c, "bk_c": bk_c, "bv_c": bv_c, "bo_r": bo_r,
            "w1_r": w1_r, "w2_r": w2_r, "w3t": w3t,
            "cos_hi": f(cos_hi_a[:, sl]), "cos_lo": f(cos_lo_a[:, sl]),
            "sin_hi": f(sinm_hi_a[:, sl]), "sin_lo": f(sinm_lo_a[:, sl]),
        })
    return nc, in_maps


def kernel(**inputs):
    nc, in_maps = prepare_in_maps(**inputs)
    res = run_bass_kernel_spmd(nc, in_maps, core_ids=list(range(NCORES)))
    out = np.concatenate([res.results[c]["y"] for c in range(NCORES)], axis=0)
    return out.reshape(1, S, D)


if __name__ == "__main__":
    nc = build_core_program()
    print("built ok, instructions:", nc.n_instructions())


# version 09 (final)
# speedup vs baseline: 1.0341x; 1.0341x over previous
import math
import numpy as np
import ml_dtypes
import concourse.bass as bass
import concourse.mybir as mybir
from concourse.bass_utils import run_bass_kernel_spmd

F32 = mybir.dt.float32
BF16 = mybir.dt.bfloat16
U8 = mybir.dt.uint8
AF = mybir.ActivationFunctionType
ALU = mybir.AluOpType
AX = mybir.AxisListType

NCORES = 8
S = 8192
D = 2048
H = 16
FF = 5632
T = S // NCORES
QT = T // 128
NB = S // 128
HC = FF // 128
DC = D // 128
CSC = 1.0 / math.sqrt(128.0)
EPS_LN = 1e-5
EPS_BL = 1e-8
NW = float(FF * D)

K_OFF = 0
K_SZ = H * 128 * T
V_OFF = K_OFF + K_SZ
V_SZ = H * 128 * QT * 128
N_OFF = V_OFF + V_SZ
PAYLOAD = N_OFF + 64



class _Track:
    _NOREC = {"wait_ge", "wait_op", "sem_inc", "maybe_wait_ge"}

    def __init__(self, eng):
        object.__setattr__(self, "_e", eng)
        object.__setattr__(self, "last", None)

    def __getattr__(self, name):
        f = getattr(object.__getattribute__(self, "_e"), name)
        if callable(f) and not name.startswith("_"):
            def wrap(*a, __f=f, __n=name, **k):
                r = __f(*a, **k)
                if __n not in _Track._NOREC and hasattr(r, "then_inc"):
                    object.__setattr__(self, "last", r)
                return r
            return wrap
        return f


def build_core_program(debug=False):
    nc = bass.Bass("TRN2", target_bir_lowering=False, detect_race_conditions=False)

    x_in = nc.dram_tensor("x", [T, D], F32, kind="ExternalInput").ap()
    wq_r = nc.dram_tensor("wq_r", [DC, 128, DC, 128], F32, kind="ExternalInput").ap()
    wk_r = nc.dram_tensor("wk_r", [DC, 128, DC, 128], F32, kind="ExternalInput").ap()
    wv_r = nc.dram_tensor("wv_r", [DC, 128, DC, 128], F32, kind="ExternalInput").ap()
    wo_r = nc.dram_tensor("wo_r", [4, H, 128, 512], F32, kind="ExternalInput").ap()
    bq_c = nc.dram_tensor("bq_c", [128, DC], F32, kind="ExternalInput").ap()
    bk_c = nc.dram_tensor("bk_c", [128, DC], F32, kind="ExternalInput").ap()
    bv_c = nc.dram_tensor("bv_c", [128, DC], F32, kind="ExternalInput").ap()
    bo_r = nc.dram_tensor("bo_r", [1, D], F32, kind="ExternalInput").ap()
    w1_r = nc.dram_tensor("w1_r", [HC, 128, DC, 128], F32, kind="ExternalInput").ap()
    w2_r = nc.dram_tensor("w2_r", [HC, 128, DC, 128], F32, kind="ExternalInput").ap()
    w3t = nc.dram_tensor("w3t", [FF, D], F32, kind="ExternalInput").ap()
    cos_hi = nc.dram_tensor("cos_hi", [128, T], F32, kind="ExternalInput").ap()
    cos_lo = nc.dram_tensor("cos_lo", [128, T], F32, kind="ExternalInput").ap()
    sin_hi = nc.dram_tensor("sin_hi", [128, T], F32, kind="ExternalInput").ap()
    sin_lo = nc.dram_tensor("sin_lo", [128, T], F32, kind="ExternalInput").ap()
    y_out = nc.dram_tensor("y", [T, D], F32, kind="ExternalOutput").ap()
    if debug:
        dbg_lnx = nc.dram_tensor("dbg_lnx", [128, DC * T], BF16, kind="ExternalOutput").ap()
        dbg_q = nc.dram_tensor("dbg_q", [128, H * T], BF16, kind="ExternalOutput").ap()
        dbg_pay = nc.dram_tensor("dbg_pay", [1, PAYLOAD], BF16, kind="ExternalOutput").ap()
        dbg_att = nc.dram_tensor("dbg_att", [128, H * T], BF16, kind="ExternalOutput").ap()
        dbg_x1 = nc.dram_tensor("dbg_x1", [T, D], F32, kind="ExternalOutput").ap()
        dbg_lnx0 = nc.dram_tensor("dbg_lnx0", [128, DC * T], BF16, kind="ExternalOutput").ap()
        dbg_qlin0 = nc.dram_tensor("dbg_qlin0", [128, T], BF16, kind="ExternalOutput").ap()
        dbg_mneg = nc.dram_tensor("dbg_mneg", [128, H], F32, kind="ExternalOutput").ap()
        dbg_qe = nc.dram_tensor("dbg_qe", [128, H * T], BF16, kind="ExternalOutput").ap()
        dbg_tab = nc.dram_tensor("dbg_tab", [128, 2 * T], F32, kind="ExternalOutput").ap()
        dbg_qsw = nc.dram_tensor("dbg_qsw", [128, T], BF16, kind="ExternalOutput").ap()
        dbg_scr = nc.dram_tensor("dbg_scr", [128, 2 * T], F32, kind="ExternalOutput").ap()
        dbg_tau = nc.dram_tensor("dbg_tau", [128, 8], F32, kind="ExternalOutput").ap()
        dbg_grow = nc.dram_tensor("dbg_grow", [1, 512], F32, kind="ExternalOutput").ap()
        dbg_gacc = nc.dram_tensor("dbg_gacc", [128, 8], F32, kind="ExternalOutput").ap()
        dbg_jd = nc.dram_tensor("dbg_jd", [128, 132], F32, kind="ExternalOutput").ap()
        dbg_q1 = nc.dram_tensor("dbg_q1", [128, D], BF16, kind="ExternalOutput").ap()
        dbg_q3 = nc.dram_tensor("dbg_q3", [128, 512], BF16, kind="ExternalOutput").ap()
        dbg_rows = nc.dram_tensor("dbg_rows", [1, 4 * 16 + 128], F32, kind="ExternalOutput").ap()
        dbg_sums = nc.dram_tensor("dbg_sums", [1, T], F32, kind="ExternalOutput").ap()
        dbg_rbc = nc.dram_tensor("dbg_rbc", [128, T], F32, kind="ExternalOutput").ap()

    cc_in = nc.dram_tensor("cc_in", [1, PAYLOAD], BF16, kind="Internal").ap()
    cc_out = nc.dram_tensor("cc_out", [NCORES, PAYLOAD], BF16, kind="Internal",
                            addr_space="Shared").ap()
    q1_scr = nc.dram_tensor("q1_scr", [HC, 128, D], BF16, kind="Internal").ap()
    q2_scr = nc.dram_tensor("q2_scr", [HC, 128, D], BF16, kind="Internal").ap()
    q3_scr = nc.dram_tensor("q3_scr", [4, HC, 128, 512], BF16, kind="Internal").ap()
    x1_scr = nc.dram_tensor("x1_scr", [T, D], F32, kind="Internal").ap()

    ident = nc.inline_tensor(np.eye(128, dtype=ml_dtypes.bfloat16), name="ident_bf").ap()

    def A(name, shape, dt):
        return nc.alloc_sbuf_tensor(name, shape, dt).ap()
    id_sb = A("id_sb", [128, 128], BF16)
    ones_cb = A("ones_cb", [128, 1], BF16)
    ones_rf = A("ones_rf", [1, 128], F32)
    bqc_sb = A("bqc_sb", [128, DC], F32)
    bkc_sb = A("bkc_sb", [128, DC], F32)
    bvc_sb = A("bvc_sb", [128, DC], F32)
    q_rot = A("q_rot", [128, H * T], BF16)
    pt_sb = A("pt_sb", [128, 2 * T], BF16)
    recip_bc = A("recip_bc", [128, T], F32)
    sums_f = A("sums_f", [1, T], F32)
    ln_row = A("ln_row", [1, T], F32)
    qmax_sb = A("qmax_sb", [1, H], F32)
    kmax_sb = A("kmax_sb", [1, H], F32)
    kml_bf = A("kml_bf", [1, 64], BF16)
    kg_bf = A("kg_bf", [1, NCORES * H], BF16)
    kgmax_sb = A("kgmax_sb", [1, H], F32)
    row_a = A("row_a", [1, 16], F32)
    row_b = A("row_b", [1, 16], F32)
    mrow = A("mrow", [1, H], F32)
    mneg_bc = A("mneg_bc", [128, H], F32)
    g_acc = A("g_acc", [128, 8], F32)
    g_part = [A("g_part0", [128, 1], F32), A("g_part1", [128, 1], F32)]
    jd_sb = A("jd_sb", [128, 132], F32) if debug else None
    grow = A("grow", [1, 1024], F32)
    tau4 = A("tau4", [1, 4], F32)
    tau_row = A("tau_row", [1, 8], F32)
    tau_bc = A("tau_bc", [128, 8], F32)
    eps_pm = A("eps_pm", [1, 2], F32)
    eps_ln_bc = A("eps_ln_bc", [128, 1], F32)
    st1 = A("st1", [128, 1], F32)
    st2 = A("st2", [128, 1], F32)
    st3 = A("st3", [128, 1], F32)
    st_m = A("st_m", [128, 1], F32)
    st_rs = A("st_rs", [128, 1], F32)
    st_vpe = A("st_vpe", [128, 1], F32)
    st_sq = A("st_sq", [128, 1], F32)

    AR1 = nc.alloc_sbuf_tensor("AR1", [128, 49152], U8)
    AR2 = nc.alloc_sbuf_tensor("AR2", [128, 32768], U8)
    AR3 = nc.alloc_sbuf_tensor("AR3", [128, 65536], U8)

    def vf(ar, off, cols):
        return ar.ap().bitcast(F32)[:, off // 4: off // 4 + cols]

    def vb(ar, off, cols):
        return ar.ap().bitcast(BF16)[:, off // 2: off // 2 + cols]

    cos_t = vf(AR1, 0, T)
    sinm_t = vf(AR1, 4096, T)
    qlin = [vb(AR1, 8192 + i * 2048, T) for i in range(2)]
    qsw = [vb(AR1, 12288 + i * 2048, T) for i in range(2)]
    vlin = [vb(AR1, 16384 + i * 2048, T) for i in range(2)]
    vst_h = [vb(AR1, 20480 + i * 2048, T) for i in range(2)]
    krot_st = [vb(AR1, 24576 + i * 2048, T) for i in range(2)]
    sq_st = [vb(AR1, 28672 + i * 2048, T) for i in range(2)]
    lo_t = vf(AR1, 32768, T)
    lo_t2 = vf(AR1, 36864, T)
    att_fm = [vb(AR1, h * 2048, T) for h in range(H)]
    lnx2_st = [vb(AR1, 32768 + i * 4096, D) for i in range(2)]
    bo_bc = vf(AR1, 40960, D)
    y3_st = [vb(AR1, qc * 4096, D) for qc in range(4)]
    x1_t = [vf(AR1, 16384 + i * 8192, D) for i in range(2)]
    scr5 = vf(AR1, 32768, D)

    lnx_fm = [vb(AR2, i * 2048, T) for i in range(DC)]
    qz_in = [vf(AR2, i * 8192, D) for i in range(2)]
    qz_out = [vb(AR2, 16384 + i * 4096, D) for i in range(2)]
    qz_a = vb(AR2, 24576, D)

    x_t = [vf(AR3, i * 8192, D) for i in range(2)]
    wqkv_b = [vb(AR3, 16384 + i * 4096, D) for i in range(2)]
    scr = vf(AR3, 24576, D)
    lnx_st = [vb(AR3, 32768 + i * 4096, D) for i in range(2)]
    k_sb = [vb(AR3, i * 16384, S) for i in range(2)]
    v_sb = [vb(AR3, 32768 + i * 16384, S) for i in range(2)]
    oy_st = [vb(AR3, qt * 4096, D) for qt in range(QT)]
    x4_t = [vf(AR3, 32768 + i * 8192, D) for i in range(2)]
    wo_b = [vb(AR3, 49152 + i * 1024, 512) for i in range(2)]
    scr4 = vf(AR3, 51200, D)
    hid_sb = [vb(AR3, hc * 1024, 512) for hc in range(HC)]
    q1_b = [vb(AR3, 45056 + i * 4096, D) for i in range(2)]
    q2_b = [vb(AR3, 53248 + i * 4096, D) for i in range(2)]
    q3_b = [vb(AR3, 61440 + i * 1024, 512) for i in range(2)]
    sig_sb = [vb(AR3, 63488 + i * 1024, 512) for i in range(2)]

    AR2_bf3 = AR2.ap().bitcast(BF16).rearrange("p (c f) -> p c f", f=1024)
    AR1_bf3 = AR1.ap().bitcast(BF16).rearrange("p (c f) -> p c f", f=1024)

    psA = nc.alloc_psum_tensor("PS_A", [128, 2048], F32).ap()
    psB = nc.alloc_psum_tensor("PS_B", [128, 1024], F32).ap()
    psC = nc.alloc_psum_tensor("PS_C", [128, 1024], F32).ap()
    psB_bf = psB.bitcast(BF16)

    single = ["s_small", "s_ln0", "s_trp", "s_trev", "s_mm1p", "s_ev1",
              "s_rope", "s_vt", "s_vtev", "s_npay", "s_sq", "s_nrm", "s_nmax",
              "s_cc", "s_kg", "s_bp", "s_bound", "s_bo", "s_mnb", "s_taub",
              "s_sc", "s_exp", "s_pv", "s_rec", "s_bc", "s_hd", "s_gamv",
              "s_gam", "s_tau", "s_qz", "s_omm", "s_oev", "s_ln1", "s_tr2p",
              "s_tr2ev", "s_m12", "s_sig", "s_hid", "s_m3", "s_y3", "s_outv",
              "s_lnv", "s_lns"]
    pairs = ["s_x", "s_wdma", "s_qswA", "s_qswB", "s_kpay", "s_vpay", "s_k",
             "s_v", "s_gw", "s_qzw", "s_qzo", "s_x4", "s_wo", "s_x1s",
             "s_q1", "s_q2", "s_q3", "s_x1l", "s_out"]
    SM = {n: nc.alloc_semaphore(n) for n in single}
    for n in pairs:
        SM[n] = [nc.alloc_semaphore(n + "0"), nc.alloc_semaphore(n + "1")]

    def dwait(eng, pair, t):
        eng.wait_ge(pair[t % 2], 16 * (t // 2 + 1))

    with nc.Block() as block:

        @block.sync
        def _(sy):
            sy = _Track(sy)
            sy.dma_start(out=id_sb, in_=ident[:, :]).then_inc(SM["s_small"], 16)
            sy.dma_start(out=bqc_sb, in_=bq_c[:, :]).then_inc(SM["s_small"], 16)
            sy.dma_start(out=bkc_sb, in_=bk_c[:, :]).then_inc(SM["s_small"], 16)
            sy.dma_start(out=bvc_sb, in_=bv_c[:, :]).then_inc(SM["s_small"], 16)
            sy.dma_start(out=cos_t, in_=cos_hi[:, :]).then_inc(SM["s_small"], 16)
            sy.dma_start(out=sinm_t, in_=sin_hi[:, :]).then_inc(SM["s_small"], 16)
            sy.dma_start(out=lo_t, in_=cos_lo[:, :]).then_inc(SM["s_small"], 16)
            sy.dma_start(out=lo_t2, in_=sin_lo[:, :]).then_inc(SM["s_small"], 16)
            for t in range(QT):
                if t >= 2:
                    sy.wait_ge(SM["s_ln0"], t - 1)
                sy.dma_start(out=x_t[t % 2],
                             in_=x_in[t * 128:(t + 1) * 128, :]).then_inc(SM["s_x"][t % 2], 16)
            if debug:
                sy.wait_ge(SM["s_trev"], 32)
                for i in range(DC):
                    sy.dma_start(out=dbg_lnx0[:, i * T:(i + 1) * T], in_=lnx_fm[i]).then_inc(SM["s_npay"], 16)
                sy.wait_ge(SM["s_ev1"], 1)
                sy.dma_start(out=dbg_qlin0, in_=qlin[0]).then_inc(SM["s_npay"], 16)
            for idx in range(2 * DC):
                pp = idx % 2
                sy.wait_ge(SM["s_ev1"], idx + 1)
                if idx >= 2:
                    sy.wait_ge(SM["s_rope"], idx - 1)
                sy.dma_start(out=qsw[pp][0:64, :], in_=qlin[pp][64:128, :]).then_inc(SM["s_qswA"][pp], 16)
                sy.dma_start(out=qsw[pp][64:128, :], in_=qlin[pp][0:64, :]).then_inc(SM["s_qswB"][pp], 16)
                if idx >= DC:
                    h = idx - DC
                    sy.wait_ge(SM["s_rope"], DC + h + 1)
                    sy.dma_start(
                        out=cc_in[0:1, K_OFF + h * 128 * T:K_OFF + (h + 1) * 128 * T]
                            .rearrange("one (p t) -> (one p) t", p=128),
                        in_=krot_st[h % 2]).then_inc(SM["s_kpay"][h % 2], 16)
            sy.wait_ge(SM["s_rope"], DC + H)
            sy.dma_start(
                out=cc_in[0:1, K_OFF + (H - 1) * 128 * T:K_OFF + H * 128 * T]
                    .rearrange("one (p t) -> (one p) t", p=128),
                in_=krot_st[(H - 1) % 2]).then_inc(SM["s_kpay"][(H - 1) % 2], 16)
            for dc in range(H):
                sy.wait_ge(SM["s_vtev"], 2 * (dc + 1))
                sy.dma_start(
                    out=cc_in[0:1, V_OFF + dc * 128 * T:V_OFF + (dc + 1) * 128 * T]
                        .rearrange("one (p f) -> (one p) f", p=128),
                    in_=vst_h[dc % 2]).then_inc(SM["s_vpay"][dc % 2], 16)
            sy.wait_ge(SM["s_nmax"], 2 * H + 1)
            sy.dma_start(out=cc_in[0:1, N_OFF:N_OFF + 64], in_=kml_bf).then_inc(SM["s_npay"], 16)
            sy.dma_start(out=bo_bc[0:1, :], in_=bo_r[:, :]).then_inc(SM["s_bo"], 16)
            p, step = 1, 1
            while p < 128:
                sy.wait_ge(SM["s_bo"], 16 * step)
                sy.dma_start(out=bo_bc[p:2 * p, :], in_=bo_bc[0:p, :]).then_inc(SM["s_bo"], 16)
                p *= 2
                step += 1
            sy.wait_ge(SM["s_cc"], 1)
            if debug:
                sy.dma_start(out=dbg_qe, in_=q_rot).then_inc(SM["s_npay"], 16)
                sy.dma_start(out=dbg_tab[:, 0:T], in_=cos_t).then_inc(SM["s_npay"], 16)
                sy.dma_start(out=dbg_tab[:, T:2 * T], in_=sinm_t).then_inc(SM["s_npay"], 16)
                sy.dma_start(out=dbg_qsw, in_=qsw[0]).then_inc(SM["s_npay"], 16)
                sy.dma_start(out=dbg_scr, in_=scr[:, 0:2 * T]).then_inc(SM["s_npay"], 16)
            sy.dma_start(out=kg_bf.rearrange("one (c h) -> one c h", c=NCORES),
                         in_=cc_out[:, N_OFF:N_OFF + H].unsqueeze(0)
                         ).then_inc(SM["s_kg"], 16)
            gci = [0]

            def sy_gamma_loads(n):
                for _ in range(n):
                    ci = gci[0]
                    if ci >= 3 * HC:
                        return
                    wi, ch = ci // HC, ci % HC
                    if ci >= 2:
                        sy.wait_ge(SM["s_gamv"], ci - 1)
                    src_w = [w1_r, w2_r, w3t][wi]
                    if wi < 2:
                        sy.dma_start(out=qz_in[ci % 2].rearrange("p (i c) -> p i c", i=DC),
                                     in_=src_w[ch, :, :, :]).then_inc(SM["s_gw"][ci % 2], 16)
                    else:
                        sy.dma_start(out=qz_in[ci % 2],
                                     in_=src_w[ch * 128:(ch + 1) * 128, :]).then_inc(SM["s_gw"][ci % 2], 16)
                    gci[0] += 1

            for h in range(H):
                if h >= 2:
                    sy.wait_ge(SM["s_hd"], h - 1)
                sy.dma_start(
                    out=k_sb[h % 2].rearrange("p (c t) -> p c t", c=NCORES),
                    in_=cc_out[:, K_OFF + h * 128 * T:K_OFF + (h + 1) * 128 * T]
                        .rearrange("c (p t) -> p c t", p=128)).then_inc(SM["s_k"][h % 2], 16)
                sy.dma_start(
                    out=v_sb[h % 2].rearrange("p (c lb f) -> p c lb f", c=NCORES, lb=QT),
                    in_=cc_out[:, V_OFF + h * 128 * QT * 128:V_OFF + (h + 1) * 128 * QT * 128]
                        .rearrange("c (p lb f) -> p c lb f", p=128, lb=QT)).then_inc(SM["s_v"][h % 2], 16)
                sy_gamma_loads(17)
            sy_gamma_loads(3 * HC)
            sy.wait_ge(SM["s_hd"], H - 1)
            for t in range(QT):
                if t >= 2:
                    dwait(sy, SM["s_x1s"], t - 2)
                sy.dma_start(out=x4_t[t % 2],
                             in_=x_in[t * 128:(t + 1) * 128, :]).then_inc(SM["s_x4"][t % 2], 16)
                if t >= 1:
                    ts = t - 1
                    sy.wait_ge(SM["s_ln1"], ts + 1)
                    sy.dma_start(out=x1_scr[ts * 128:(ts + 1) * 128, :],
                                 in_=x4_t[ts % 2]).then_inc(SM["s_x1s"][ts % 2], 16)
            sy.wait_ge(SM["s_ln1"], QT)
            sy.dma_start(out=x1_scr[(QT - 1) * 128:QT * 128, :],
                         in_=x4_t[(QT - 1) % 2]).then_inc(SM["s_x1s"][(QT - 1) % 2], 16)
            sy.wait_ge(SM["s_ln1"], QT)

            def x1_reload(t):
                if t >= 2:
                    dwait(sy, SM["s_out"], t - 2)
                sy.dma_start(out=x1_t[t % 2],
                             in_=x1_scr[t * 128:(t + 1) * 128, :]).then_inc(SM["s_x1l"][t % 2], 16)

            sy.wait_ge(SM["s_x1s"][0], 16 * (QT // 2))
            sy.wait_ge(SM["s_x1s"][1], 16 * (QT // 2))
            sy.wait_ge(SM["s_omm"], 4 * H)
            x1_reload(0)
            x1_reload(1)
            for half in range(2):
                for hc in range(HC):
                    idx = half * HC + hc
                    if idx >= 2:
                        sy.wait_ge(SM["s_m12"], 2 * (idx - 1))
                    sy.dma_start(out=q1_b[idx % 2], in_=q1_scr[hc, :, :]).then_inc(SM["s_q1"][idx % 2], 16)
                    sy.dma_start(out=q2_b[idx % 2], in_=q2_scr[hc, :, :]).then_inc(SM["s_q2"][idx % 2], 16)
                if half == 1:
                    x1_reload(4)
                    x1_reload(5)
                for dq in range(4):
                    for hc in range(HC):
                        idx3 = (half * 4 + dq) * HC + hc
                        if idx3 >= 2:
                            sy.wait_ge(SM["s_m3"], idx3 - 1)
                        sy.dma_start(out=q3_b[idx3 % 2],
                                     in_=q3_scr[dq, hc, :, :]).then_inc(SM["s_q3"][idx3 % 2], 16)
                if half == 0:
                    x1_reload(2)
                    x1_reload(3)
            x1_reload(6)
            x1_reload(7)
            sy.wait_ge(SM["s_out"][0], 16 * (QT // 2))
            sy.wait_ge(SM["s_out"][1], 16 * (QT // 2))
            if debug:
                sy.wait_ge(SM["s_npay"], 16 * (1 + DC + 1 + 3 + H + 1 + DC + 1 + 7 + 5 + 4))

        @block.gpsimd
        def _(g):
            g = _Track(g)
            for wi, wr in enumerate([wq_r, wk_r, wv_r]):
                for dc in range(DC):
                    idx = wi * DC + dc
                    if idx >= 2:
                        g.wait_ge(SM["s_mm1p"], idx - 1)
                    g.dma_start(out=wqkv_b[idx % 2].rearrange("p (i c) -> p i c", i=DC),
                                in_=wr[dc, :, :, :]).then_inc(SM["s_wdma"][idx % 2], 16)
            g.wait_ge(SM["s_kpay"][0], 16 * (H // 2))
            g.wait_ge(SM["s_kpay"][1], 16 * (H // 2))
            g.wait_ge(SM["s_vpay"][0], 16 * (H // 2))
            g.wait_ge(SM["s_vpay"][1], 16 * (H // 2))
            g.wait_ge(SM["s_npay"], 16)
            g.collective_compute(
                "AllGather", ALU.bypass, replica_groups=[list(range(NCORES))],
                ins=[cc_in[:, :]], outs=[cc_out[:, :]],
            ).then_inc(SM["s_cc"], 1)
            g.wait_ge(SM["s_bound"], 1)
            g.dma_start(out=mneg_bc[0:1, :], in_=mrow).then_inc(SM["s_mnb"], 16)
            p, stp = 1, 1
            while p < 128:
                g.wait_ge(SM["s_mnb"], 16 * stp)
                g.dma_start(out=mneg_bc[p:2 * p, :], in_=mneg_bc[0:p, :]).then_inc(SM["s_mnb"], 16)
                p *= 2
                stp += 1
            g.wait_ge(SM["s_gamv"], 3 * HC + 1)
            for wq_i in range(8):
                g.dma_start(out=grow[0:1, wq_i * 128:(wq_i + 1) * 128],
                            in_=g_acc[:, wq_i:wq_i + 1]).then_inc(SM["s_gam"], 16)
            g.wait_ge(SM["s_tau"], 1)
            g.dma_start(out=tau_bc[0:1, :], in_=tau_row).then_inc(SM["s_taub"], 16)
            p, stp = 1, 1
            while p < 128:
                g.wait_ge(SM["s_taub"], 16 * stp)
                g.dma_start(out=tau_bc[p:2 * p, :], in_=tau_bc[0:p, :]).then_inc(SM["s_taub"], 16)
                p *= 2
                stp += 1
            for wi, (src, dst) in enumerate([(w1_r, q1_scr), (w2_r, q2_scr), (w3t, q3_scr)]):
                for ch in range(HC):
                    idx = wi * HC + ch
                    if idx >= 2:
                        g.wait_ge(SM["s_qz"], idx - 1)
                    if wi < 2:
                        g.dma_start(out=qz_in[idx % 2].rearrange("p (i c) -> p i c", i=DC),
                                    in_=src[ch, :, :, :]).then_inc(SM["s_qzw"][idx % 2], 16)
                    else:
                        g.dma_start(out=qz_in[idx % 2],
                                    in_=src[ch * 128:(ch + 1) * 128, :]).then_inc(SM["s_qzw"][idx % 2], 16)
                    g.wait_ge(SM["s_qz"], idx + 1)
                    if wi < 2:
                        g.dma_start(out=dst[ch, :, :], in_=qz_out[idx % 2]).then_inc(SM["s_qzo"][idx % 2], 16)
                    else:
                        g.dma_start(out=dst.rearrange("dq hc p c -> p dq hc c")[:, :, ch, :],
                                    in_=qz_out[idx % 2].rearrange("p (dq c) -> p dq c", dq=4)
                                    ).then_inc(SM["s_qzo"][idx % 2], 16)
            for dq in range(4):
                for h in range(H):
                    idx = dq * H + h
                    if idx >= 2:
                        g.wait_ge(SM["s_omm"], idx - 1)
                    g.dma_start(out=wo_b[idx % 2], in_=wo_r[dq, h, :, :]).then_inc(SM["s_wo"][idx % 2], 16)

        @block.tensor
        def _(t):
            t = _Track(t)
            t.wait_ge(SM["s_small"], 16 * 8)
            for tile in range(QT):
                t.wait_ge(SM["s_ln0"], tile + 1)
                for grp in range(4):
                    gi = tile * 4 + grp
                    if gi >= 2:
                        t.wait_ge(SM["s_trev"], gi - 1)
                    for j in range(4):
                        dcc = grp * 4 + j
                        t.transpose(psB_bf[:, (gi % 2) * 1024 + j * 128:(gi % 2) * 1024 + j * 128 + 128],
                                    lnx_st[tile % 2][:, dcc * 128:(dcc + 1) * 128],
                                    id_sb)
                    t.last.then_inc(SM["s_trp"], 1)
            sqidx = 0
            first_vt = True
            for wi in range(3):
                for dc in range(DC):
                    idx = wi * DC + dc
                    dwait(t, SM["s_wdma"], idx)
                    if idx == 0:
                        t.wait_ge(SM["s_trev"], 32)
                    if idx >= 2:
                        t.wait_ge(SM["s_ev1"], idx - 1)
                    pp = idx % 2
                    for i in range(DC):
                        for n2 in range(2):
                            t.matmul(psA[:, pp * 1024 + n2 * 512: pp * 1024 + (n2 + 1) * 512],
                                     wqkv_b[pp][:, i * 128:(i + 1) * 128],
                                     lnx_fm[i][:, n2 * 512:(n2 + 1) * 512],
                                     start=(i == 0), stop=(i == DC - 1))
                    t.last.then_inc(SM["s_mm1p"], 1)
                    if wi == 2:
                        t.wait_ge(SM["s_ev1"], idx + 1)
                        if first_vt:
                            t.wait_ge(SM["s_trev"], 32)
                            first_vt = False
                        for grp in range(2):
                            vgi = dc * 2 + grp
                            if vgi >= 2:
                                t.wait_ge(SM["s_vtev"], vgi - 1)
                            for j in range(4):
                                tc = grp * 4 + j
                                t.transpose(psB_bf[:, (vgi % 2) * 1024 + j * 128:(vgi % 2) * 1024 + j * 128 + 128],
                                            vlin[dc % 2][:, tc * 128:(tc + 1) * 128],
                                            id_sb)
                            t.last.then_inc(SM["s_vt"], 1)
                    else:
                        t.wait_ge(SM["s_sq"], sqidx + 1)
                        if sqidx >= 1:
                            t.wait_ge(SM["s_nmax"], sqidx)
                        for n2 in range(2):
                            t.matmul(psC[0:1, n2 * 512:(n2 + 1) * 512],
                                     ones_cb, sq_st[sqidx % 2][:, n2 * 512:(n2 + 1) * 512],
                                     start=True, stop=True)
                        t.last.then_inc(SM["s_nrm"], 1)
                        sqidx += 1
            for h in range(H):
                dwait(t, SM["s_k"], h)
                dwait(t, SM["s_v"], h)
                if h == 0:
                    t.wait_ge(SM["s_nmax"], 2 * H)
                for b in range(NB):
                    idx = h * NB + b
                    pp = idx % 2
                    if idx >= 2:
                        t.wait_ge(SM["s_exp"], idx - 1)
                    for n2 in range(2):
                        t.matmul(psA[:, pp * 1024 + n2 * 512: pp * 1024 + (n2 + 1) * 512],
                                 k_sb[h % 2][:, b * 128:(b + 1) * 128],
                                 q_rot[:, h * T + n2 * 512: h * T + (n2 + 1) * 512],
                                 start=True, stop=True)
                    t.last.then_inc(SM["s_sc"], 1)
                    t.wait_ge(SM["s_exp"], idx + 1)
                    if b == 0 and h > 0:
                        t.wait_ge(SM["s_hd"], h)
                    for n2 in range(2):
                        t.matmul(psB[:, n2 * 512:(n2 + 1) * 512],
                                 v_sb[h % 2][:, b * 128:(b + 1) * 128],
                                 pt_sb[:, pp * T + n2 * 512: pp * T + (n2 + 1) * 512],
                                 start=(b == 0), stop=(b == NB - 1))
                    for n2 in range(2):
                        t.matmul(psC[0:1, n2 * 512:(n2 + 1) * 512],
                                 ones_cb,
                                 pt_sb[:, pp * T + n2 * 512: pp * T + (n2 + 1) * 512],
                                 start=(b == 0), stop=(b == NB - 1))
                    t.last.then_inc(SM["s_pv"], 1)
                t.wait_ge(SM["s_rec"], h + 1)
                for n2 in range(2):
                    t.matmul(psC[:, n2 * 512:(n2 + 1) * 512],
                             ones_rf, sums_f[0:1, n2 * 512:(n2 + 1) * 512],
                             start=True, stop=True)
                t.last.then_inc(SM["s_bc"], 1)
            ps_o = [psA[:, i * 512:(i + 1) * 512] for i in range(4)] + \
                   [psB[:, i * 512:(i + 1) * 512] for i in range(2)] + \
                   [psC[:, i * 512:(i + 1) * 512] for i in range(2)]
            t.wait_ge(SM["s_hd"], H)
            t.wait_ge(SM["s_exp"], H * NB)
            for dq in range(4):
                for h in range(H):
                    idx = dq * H + h
                    dwait(t, SM["s_wo"], idx)
                    if h == 0 and dq > 0:
                        t.wait_ge(SM["s_oev"], dq * QT)
                    for qt in range(QT):
                        t.matmul(ps_o[qt],
                                 att_fm[h][:, qt * 128:(qt + 1) * 128],
                                 wo_b[idx % 2],
                                 start=(h == 0), stop=(h == H - 1))
                    t.last.then_inc(SM["s_omm"], 1)
            t.wait_ge(SM["s_oev"], 4 * QT)
            for tile in range(QT):
                t.wait_ge(SM["s_ln1"], tile + 1)
                for grp in range(4):
                    gi = tile * 4 + grp
                    if gi >= 2:
                        t.wait_ge(SM["s_tr2ev"], gi - 1)
                    for j in range(4):
                        dcc = grp * 4 + j
                        t.transpose(psB_bf[:, (gi % 2) * 1024 + j * 128:(gi % 2) * 1024 + j * 128 + 128],
                                    lnx2_st[tile % 2][:, dcc * 128:(dcc + 1) * 128],
                                    id_sb)
                    t.last.then_inc(SM["s_tr2p"], 1)
            t.wait_ge(SM["s_tr2ev"], 32)
            for half in range(2):
                for hc in range(HC):
                    idx = half * HC + hc
                    dwait(t, SM["s_q1"], idx)
                    dwait(t, SM["s_q2"], idx)
                    if idx >= 2:
                        t.wait_ge(SM["s_sig"], idx - 1)
                        t.wait_ge(SM["s_hid"], idx - 1)
                    for i in range(DC):
                        t.matmul(psB[:, (idx % 2) * 512:((idx % 2) + 1) * 512],
                                 q1_b[idx % 2][:, i * 128:(i + 1) * 128],
                                 lnx_fm[i][:, half * 512:(half + 1) * 512],
                                 start=(i == 0), stop=(i == DC - 1))
                    for i in range(DC):
                        t.matmul(psC[:, (idx % 2) * 512:((idx % 2) + 1) * 512],
                                 q2_b[idx % 2][:, i * 128:(i + 1) * 128],
                                 lnx_fm[i][:, half * 512:(half + 1) * 512],
                                 start=(i == 0), stop=(i == DC - 1))
                    t.last.then_inc(SM["s_m12"], 2)
                for dq in range(4):
                    for hc in range(HC):
                        idx3 = (half * 4 + dq) * HC + hc
                        dwait(t, SM["s_q3"], idx3)
                        t.wait_ge(SM["s_hid"], half * HC + hc + 1)
                        if hc == 0 and (half * 4 + dq) > 0:
                            t.wait_ge(SM["s_y3"], half * 4 + dq)
                        for qc in range(4):
                            t.matmul(psA[:, qc * 512:(qc + 1) * 512],
                                     hid_sb[hc][:, qc * 128:(qc + 1) * 128],
                                     q3_b[idx3 % 2],
                                     start=(hc == 0), stop=(hc == HC - 1))
                        t.last.then_inc(SM["s_m3"], 1)

        def ln_service(sc):
            inv_n = 1.0 / D
            sc.activation(st3, st1, AF.Square, scale=inv_n)
            sc.activation(st_vpe, st3, AF.Identity, scale=-1.0,
                          bias=eps_ln_bc[:, 0:1])
            sc.activation(st_m, st1, AF.Identity, scale=-inv_n)
            sc.activation(st_sq, st2, AF.Ln, scale=inv_n, bias=st_vpe)
            sc.activation(st_rs, st_sq, AF.Exp, scale=-0.5)
            sc.last.then_inc(SM["s_lns"], 1)

        @block.scalar
        def _(sc):
            sc = _Track(sc)
            for t in range(QT):
                sc.wait_ge(SM["s_lnv"], t + 1)
                ln_service(sc)
            sqidx = 0
            for wi in range(3):
                for dc in range(DC):
                    idx = wi * DC + dc
                    sc.wait_ge(SM["s_mm1p"], idx + 1)
                    pp = idx % 2
                    src_ps = psA[:, pp * 1024: pp * 1024 + 1024]
                    if wi == 0:
                        if idx >= 2:
                            dwait(sc, SM["s_qswA"], idx - 2)
                            dwait(sc, SM["s_qswB"], idx - 2)
                            sc.wait_ge(SM["s_rope"], idx - 1)
                        sc.activation(qlin[pp], src_ps, AF.Identity, bias=bqc_sb[:, dc:dc + 1])
                    elif wi == 1:
                        if idx >= 2:
                            dwait(sc, SM["s_qswA"], idx - 2)
                            dwait(sc, SM["s_qswB"], idx - 2)
                            sc.wait_ge(SM["s_rope"], idx - 1)
                        sc.activation(qlin[pp], src_ps, AF.Identity, bias=bkc_sb[:, dc:dc + 1])
                    else:
                        if dc >= 2:
                            sc.wait_ge(SM["s_vt"], 2 * (dc - 1))
                        sc.activation(vlin[dc % 2], src_ps, AF.Identity, bias=bvc_sb[:, dc:dc + 1])
                    sc.last.then_inc(SM["s_ev1"], 1)
                    if wi == 0:
                        sc.wait_ge(SM["s_rope"], dc + 1)
                        if sqidx >= 2:
                            sc.wait_ge(SM["s_nrm"], sqidx - 1)
                        sc.activation(sq_st[sqidx % 2], q_rot[:, dc * T:(dc + 1) * T], AF.Square)
                        sc.last.then_inc(SM["s_sq"], 1)
                        sqidx += 1
                    elif wi == 1:
                        sc.wait_ge(SM["s_rope"], DC + dc + 1)
                        if sqidx >= 2:
                            sc.wait_ge(SM["s_nrm"], sqidx - 1)
                        sc.activation(sq_st[sqidx % 2], krot_st[dc % 2], AF.Square)
                        sc.last.then_inc(SM["s_sq"], 1)
                        sqidx += 1
            sc.wait_ge(SM["s_bp"], 1)
            sc.activation(row_b, row_a, AF.Ln)
            sc.activation(row_b, row_b, AF.Exp, scale=0.5)
            sc.last.then_inc(SM["s_bp"], 1)
            sc.wait_ge(SM["s_mnb"], 16 * 8)
            for h in range(H):
                if h == 8:
                    sc.wait_ge(SM["s_gam"], 16 * 8)
                    for wq_i in range(4):
                        sc.activation(ln_row[0:1, 0:256],
                                      grow[0:1, wq_i * 256:(wq_i + 1) * 256],
                                      AF.Identity, accum_out=tau4[0:1, wq_i:wq_i + 1])
                    sc.activation(tau_row[0:1, 0:4], tau4[0:1, 0:4], AF.Identity,
                                  scale=0.5 / NW, bias=eps_pm[0:1, 0:1])
                    sc.activation(tau_row[0:1, 4:8], tau4[0:1, 0:4], AF.Identity,
                                  scale=-0.5 / NW, bias=eps_pm[0:1, 1:2])
                    sc.last.then_inc(SM["s_tau"], 1)
                for b in range(NB):
                    idx = h * NB + b
                    pp = idx % 2
                    sc.wait_ge(SM["s_sc"], idx + 1)
                    if idx >= 2:
                        sc.wait_ge(SM["s_pv"], idx - 1)
                    sc.activation(pt_sb[:, pp * T:(pp + 1) * T],
                                  psA[:, pp * 1024: pp * 1024 + 1024],
                                  AF.Exp, bias=mneg_bc[:, h:h + 1], scale=CSC)
                    sc.last.then_inc(SM["s_exp"], 1)
                sc.wait_ge(SM["s_pv"], NB * (h + 1))
                sc.activation(ln_row, psC[0:1, 0:1024], AF.Ln)
                sc.activation(sums_f, ln_row, AF.Exp, scale=-1.0)
                sc.last.then_inc(SM["s_rec"], 1)

            for t in range(QT):
                for k in range(2):
                    sc.wait_ge(SM["s_lnv"], QT + 2 * t + k + 1)
                    ln_service(sc)
            if debug:
                sc.wait_ge(SM["s_trev"], 32)
                for i in range(DC):
                    sc.dma_start(out=dbg_lnx[:, i * T:(i + 1) * T], in_=lnx_fm[i]).then_inc(SM["s_npay"], 16)
                sc.wait_ge(SM["s_rope"], DC)
                sc.dma_start(out=dbg_q, in_=q_rot).then_inc(SM["s_npay"], 16)
                sc.wait_ge(SM["s_cc"], 1)
                sc.dma_start(out=dbg_pay[:, K_OFF:K_OFF + K_SZ].rearrange("one (a b) -> (one a) b", a=2048),
                             in_=cc_in[:, K_OFF:K_OFF + K_SZ].rearrange("one (a b) -> (one a) b", a=2048)).then_inc(SM["s_npay"], 16)
                sc.dma_start(out=dbg_pay[:, V_OFF:V_OFF + V_SZ].rearrange("one (a b) -> (one a) b", a=2048),
                             in_=cc_in[:, V_OFF:V_OFF + V_SZ].rearrange("one (a b) -> (one a) b", a=2048)).then_inc(SM["s_npay"], 16)
                sc.dma_start(out=dbg_pay[:, N_OFF:N_OFF + 64],
                             in_=cc_in[:, N_OFF:N_OFF + 64]).then_inc(SM["s_npay"], 16)
                sc.wait_ge(SM["s_hd"], H)
                sc.dma_start(out=dbg_mneg, in_=mneg_bc).then_inc(SM["s_npay"], 16)
                sc.dma_start(out=dbg_rows[:, 0:16], in_=qmax_sb).then_inc(SM["s_npay"], 16)
                sc.dma_start(out=dbg_rows[:, 16:32], in_=kmax_sb).then_inc(SM["s_npay"], 16)
                sc.dma_start(out=dbg_rows[:, 32:48], in_=kgmax_sb).then_inc(SM["s_npay"], 16)
                sc.dma_start(out=dbg_rows[:, 48:64], in_=mrow).then_inc(SM["s_npay"], 16)

                sc.dma_start(out=dbg_sums, in_=sums_f).then_inc(SM["s_npay"], 16)
                sc.wait_ge(SM["s_qzo"][0], 16 * (3 * HC // 2))
                sc.wait_ge(SM["s_qzo"][1], 16 * (3 * HC // 2))
                sc.dma_start(out=dbg_tau, in_=tau_bc).then_inc(SM["s_npay"], 16)
                sc.dma_start(out=dbg_grow, in_=grow[0:1, 0:512]).then_inc(SM["s_npay"], 16)
                sc.dma_start(out=dbg_gacc, in_=g_acc[:, :]).then_inc(SM["s_npay"], 16)
                sc.dma_start(out=dbg_jd, in_=jd_sb[:, :]).then_inc(SM["s_npay"], 16)
                sc.dma_start(out=dbg_q1, in_=q1_scr[0, :, :]).then_inc(SM["s_npay"], 16)
                sc.dma_start(out=dbg_q3, in_=q3_scr[0, 0, :, :]).then_inc(SM["s_npay"], 16)
                sc.dma_start(out=dbg_rbc, in_=recip_bc).then_inc(SM["s_npay"], 16)
                for hh in range(H):
                    sc.dma_start(out=dbg_att[:, hh * T:(hh + 1) * T], in_=att_fm[hh]).then_inc(SM["s_npay"], 16)
                sc.wait_ge(SM["s_x1s"][0], 16 * (QT // 2))
                sc.wait_ge(SM["s_x1s"][1], 16 * (QT // 2))
                sc.dma_start(out=dbg_x1[:, :], in_=x1_scr[:, :]).then_inc(SM["s_npay"], 16)
            for half in range(2):
                for hc in range(HC):
                    idx = half * HC + hc
                    sc.wait_ge(SM["s_m12"], 2 * (idx + 1))
                    if idx >= 2:
                        sc.wait_ge(SM["s_hid"], idx - 1)
                    sc.activation(sig_sb[idx % 2],
                                  psB[:, (idx % 2) * 512:((idx % 2) + 1) * 512],
                                  AF.Sigmoid)
                    sc.last.then_inc(SM["s_sig"], 1)
                for qc in range(4):
                    t = half * 4 + qc
                    sc.wait_ge(SM["s_lnv"], 3 * QT + t + 1)
                    ln_service(sc)
                    sc.wait_ge(SM["s_outv"], t + 1)
                    sc.dma_start(out=y_out[t * 128:(t + 1) * 128, :],
                                 in_=x1_t[t % 2]).then_inc(SM["s_out"][t % 2], 16)

        @block.vector
        def _(v):
            v = _Track(v)
            lnv_count = [0]

            def ln_stats(xin, n_free, extra_rs_scale=None, dump=None):
                if dump is None:
                    dump = scr
                v.scalar_tensor_tensor(dump, xin, 1.0, xin, ALU.bypass, ALU.mult,
                                       accum_out=st2)
                v.tensor_reduce(st1, xin, AX.X, ALU.add)
                lnv_count[0] += 1
                v.last.then_inc(SM["s_lnv"], 1)
                v.wait_ge(SM["s_lns"], lnv_count[0])
                if extra_rs_scale is not None:
                    v.tensor_scalar(st_rs, st_rs, extra_rs_scale, None, ALU.mult)

            v.wait_ge(SM["s_small"], 16 * 8)
            v.tensor_tensor(cos_t, cos_t, lo_t, ALU.add)
            v.tensor_tensor(sinm_t, sinm_t, lo_t2, ALU.add)
            v.memset(ones_cb, 1.0)
            v.memset(ones_rf, 1.0)
            v.memset(kml_bf, 0.0)
            v.memset(eps_pm[0:1, 0:1], 0.5 * EPS_BL)
            v.memset(eps_pm[0:1, 1:2], -0.5 * EPS_BL)
            v.memset(eps_ln_bc, EPS_LN)
            for t in range(QT):
                dwait(v, SM["s_x"], t)
                ln_stats(x_t[t % 2], D)
                v.tensor_scalar(lnx_st[t % 2], x_t[t % 2], st_m, st_rs, ALU.add, ALU.mult)
                v.last.then_inc(SM["s_ln0"], 1)
                for grp in range(4):
                    gi = t * 4 + grp
                    v.wait_ge(SM["s_trp"], gi + 1)
                    out_ap = AR2_bf3[:, grp * 4:grp * 4 + 4, t * 128:t * 128 + 128]
                    v.tensor_copy(out_ap,
                                  psB_bf[:, (gi % 2) * 1024:(gi % 2) * 1024 + 512]
                                  .rearrange("p (c f) -> p c f", f=128))
                    v.last.then_inc(SM["s_trev"], 1)
            for wi in range(2):
                for dc in range(DC):
                    idx = wi * DC + dc
                    pp = idx % 2
                    dwait(v, SM["s_qswA"], idx)
                    dwait(v, SM["s_qswB"], idx)
                    if wi == 1 and dc >= 2:
                        dwait(v, SM["s_kpay"], dc - 2)
                        v.wait_ge(SM["s_sq"], DC + dc - 1)
                    dst = q_rot[:, dc * T:(dc + 1) * T] if wi == 0 else krot_st[dc % 2]
                    v.tensor_tensor(scr[:, 0:T], qlin[pp], cos_t, ALU.mult)
                    v.tensor_tensor(scr[:, T:2 * T], qsw[pp], sinm_t, ALU.mult)
                    v.tensor_tensor(dst, scr[:, 0:T], scr[:, T:2 * T], ALU.add)
                    v.last.then_inc(SM["s_rope"], 1)
                    v.wait_ge(SM["s_nrm"], idx + 1)
                    dstmax = qmax_sb[0:1, dc:dc + 1] if wi == 0 else kmax_sb[0:1, dc:dc + 1]
                    v.tensor_reduce(dstmax, psC[0:1, 0:1024], AX.X, ALU.max)
                    v.last.then_inc(SM["s_nmax"], 1)
            v.tensor_copy(kml_bf[0:1, 0:H], kmax_sb)
            v.last.then_inc(SM["s_nmax"], 1)
            for vgi in range(32):
                dc, grp = vgi // 2, vgi % 2
                v.wait_ge(SM["s_vt"], vgi + 1)
                if dc >= 2 and grp == 0:
                    dwait(v, SM["s_vpay"], dc - 2)
                v.tensor_copy(vst_h[dc % 2][:, grp * 512:(grp + 1) * 512],
                              psB_bf[:, (vgi % 2) * 1024:(vgi % 2) * 1024 + 512])
                v.last.then_inc(SM["s_vtev"], 1)
            v.wait_ge(SM["s_kg"], 16)
            v.tensor_copy(kgmax_sb, kg_bf[0:1, 0:H])
            for cc_i in range(1, NCORES):
                v.tensor_max(kgmax_sb, kgmax_sb,
                             kg_bf[0:1, cc_i * H:(cc_i + 1) * H])
            v.tensor_tensor(row_a, qmax_sb, kgmax_sb, ALU.mult)
            v.last.then_inc(SM["s_bp"], 1)
            v.wait_ge(SM["s_bp"], 2)
            v.tensor_scalar(mrow, row_b, -CSC, 70.0, ALU.mult, ALU.add)
            v.tensor_scalar_min(mrow, mrow, 0.0)
            v.last.then_inc(SM["s_bound"], 1)
            v.memset(g_acc, 0.0)
            job_pos = [0]
            NJOBS = 3 * HC + 3 * HC

            def emit_jobs(nmax):
                for _ in range(nmax):
                    j = job_pos[0]
                    if j >= NJOBS:
                        return
                    def gadd(pj):
                        wj = pj // HC
                        jjp = 2 * wj + (pj % 2)
                        v.tensor_tensor(g_acc[:, jjp:jjp + 1],
                                        g_acc[:, jjp:jjp + 1],
                                        g_part[pj % 2], ALU.add)
                        if pj == 3 * HC - 1:
                            v.last.then_inc(SM["s_gamv"], 1)
                        if debug:
                            v.tensor_copy(jd_sb[:, pj:pj + 1], g_part[pj % 2])

                    if j < 3 * HC:
                        ci = j
                        dwait(v, SM["s_gw"], ci)
                        v.tensor_reduce(g_part[ci % 2], qz_in[ci % 2], AX.X,
                                        ALU.add, apply_absolute_value=True)
                        v.last.then_inc(SM["s_gamv"], 1)
                        if ci >= 1:
                            gadd(ci - 1)
                    else:
                        idx = j - 3 * HC
                        wi = idx // HC
                        if idx == 0:
                            gadd(3 * HC - 1)
                            v.wait_ge(SM["s_taub"], 16 * 8)
                        dwait(v, SM["s_qzw"], idx)
                        if idx >= 2:
                            dwait(v, SM["s_qzo"], idx - 2)
                        win = qz_in[idx % 2]
                        v.tensor_scalar(qz_a, win, tau_bc[:, wi:wi + 1], 2.0,
                                        ALU.is_ge, ALU.mult)
                        v.tensor_scalar(qz_out[idx % 2], win, tau_bc[:, 4 + wi:4 + wi + 1],
                                        -2.0, ALU.is_le, ALU.mult)
                        v.tensor_tensor(qz_out[idx % 2], qz_out[idx % 2], qz_a, ALU.add)
                        v.last.then_inc(SM["s_qz"], 1)
                    job_pos[0] += 1

            for h in range(H):
                v.wait_ge(SM["s_bc"], h + 1)
                v.tensor_copy(recip_bc, psC)
                v.tensor_tensor(att_fm[h], psB, recip_bc, ALU.mult)
                v.last.then_inc(SM["s_hd"], 1)
                emit_jobs(18)
            emit_jobs(NJOBS)
            for dq in range(4):
                v.wait_ge(SM["s_omm"], dq * H + H)
                v.wait_ge(SM["s_bo"], 16 * 8)
                for qt in range(QT):
                    psrc = ([psA[:, i * 512:(i + 1) * 512] for i in range(4)] +
                            [psB[:, i * 512:(i + 1) * 512] for i in range(2)] +
                            [psC[:, i * 512:(i + 1) * 512] for i in range(2)])[qt]
                    v.tensor_tensor(oy_st[qt][:, dq * 512:(dq + 1) * 512], psrc,
                                    bo_bc[:, dq * 512:(dq + 1) * 512], ALU.add)
                    v.last.then_inc(SM["s_oev"], 1)
            v.wait_ge(SM["s_qzo"][0], 16 * (3 * HC // 2))
            v.wait_ge(SM["s_qzo"][1], 16 * (3 * HC // 2))
            for t in range(QT):
                v.wait_ge(SM["s_oev"], 4 * QT)
                dwait(v, SM["s_x4"], t)
                ln_stats(oy_st[t], D, dump=scr4)
                v.tensor_scalar(scr4, oy_st[t], st_m, st_rs, ALU.add, ALU.mult)
                v.tensor_tensor(x4_t[t % 2], x4_t[t % 2], scr4, ALU.add)
                ln_stats(x4_t[t % 2], D, extra_rs_scale=0.5, dump=scr4)
                v.tensor_scalar(lnx2_st[t % 2], x4_t[t % 2], st_m, st_rs, ALU.add, ALU.mult)
                v.last.then_inc(SM["s_ln1"], 1)
                for grp in range(4):
                    gi = t * 4 + grp
                    v.wait_ge(SM["s_tr2p"], gi + 1)
                    out_ap = AR2_bf3[:, grp * 4:grp * 4 + 4, t * 128:t * 128 + 128]
                    v.tensor_copy(out_ap,
                                  psB_bf[:, (gi % 2) * 1024:(gi % 2) * 1024 + 512]
                                  .rearrange("p (c f) -> p c f", f=128))
                    v.last.then_inc(SM["s_tr2ev"], 1)
            v.wait_ge(SM["s_x1s"][0], 16 * (QT // 2))
            v.wait_ge(SM["s_x1s"][1], 16 * (QT // 2))
            for half in range(2):
                for hc in range(HC):
                    idx = half * HC + hc
                    v.wait_ge(SM["s_sig"], idx + 1)
                    v.tensor_tensor(hid_sb[hc], sig_sb[idx % 2],
                                    psC[:, (idx % 2) * 512:((idx % 2) + 1) * 512],
                                    ALU.mult)
                    v.last.then_inc(SM["s_hid"], 1)
                for dq in range(4):
                    v.wait_ge(SM["s_m3"], (half * 4 + dq) * HC + HC)
                    for qc in range(4):
                        v.tensor_copy(y3_st[qc][:, dq * 512:(dq + 1) * 512],
                                      psA[:, qc * 512:(qc + 1) * 512])
                    v.last.then_inc(SM["s_y3"], 1)
                for qc in range(4):
                    t = half * 4 + qc
                    dwait(v, SM["s_x1l"], t)
                    ln_stats(y3_st[qc], D, dump=scr5)
                    v.tensor_scalar(scr5, y3_st[qc], st_m, st_rs, ALU.add, ALU.mult)
                    v.tensor_tensor(x1_t[t % 2], x1_t[t % 2], scr5, ALU.add)
                    v.last.then_inc(SM["s_outv"], 1)

    return nc


_TABLES = None


def _make_tables():
    global _TABLES
    if _TABLES is not None:
        return _TABLES
    HDh = 64
    pos = np.arange(S, dtype=np.float64)
    invf = 1.0 / (10000.0 ** (np.arange(0, 128, 2, dtype=np.float64) / 128.0))
    sinusoid = pos[:, None] * invf[None, :]
    emb = np.concatenate([np.sin(sinusoid), np.cos(sinusoid)], axis=-1)
    cos_e = np.cos(emb).T
    sin_e = np.sin(emb).T
    sinm = np.concatenate([-sin_e[0:HDh], sin_e[HDh:]], axis=0)

    def split(xx):
        hi = xx.astype(np.float32)
        lo = (xx - hi.astype(np.float64)).astype(np.float32)
        return np.ascontiguousarray(hi), np.ascontiguousarray(lo)

    _TABLES = (*split(cos_e), *split(sinm))
    return _TABLES


def prepare_in_maps(L0_attn_x, L0_attn_Wq, L0_attn_bq, L0_attn_Wk, L0_attn_bk,
                    L0_attn_Wv, L0_attn_bv, L0_attn_Wo, L0_attn_bo,
                    L0_ffn_w1, L0_ffn_w2, L0_ffn_w3):
    f = np.ascontiguousarray
    x = np.asarray(L0_attn_x, dtype=np.float32).reshape(S, D)
    Wq = np.asarray(L0_attn_Wq, np.float32)
    Wk = np.asarray(L0_attn_Wk, np.float32)
    Wv = np.asarray(L0_attn_Wv, np.float32)
    Wo = np.asarray(L0_attn_Wo, np.float32)
    w1 = np.asarray(L0_ffn_w1, np.float32)
    w2 = np.asarray(L0_ffn_w2, np.float32)
    w3 = np.asarray(L0_ffn_w3, np.float32)

    wq_r = f(Wq.reshape(DC, 128, DC, 128).transpose(0, 3, 2, 1))
    wk_r = f(Wk.reshape(DC, 128, DC, 128).transpose(0, 3, 2, 1))
    wv_r = f(Wv.reshape(DC, 128, DC, 128).transpose(0, 3, 2, 1))
    wo_r = f(Wo.reshape(4, 512, H, 128).transpose(0, 2, 3, 1))
    bq_c = f(np.asarray(L0_attn_bq, np.float32).reshape(DC, 128).T)
    bk_c = f(np.asarray(L0_attn_bk, np.float32).reshape(DC, 128).T)
    bv_c = f(np.asarray(L0_attn_bv, np.float32).reshape(DC, 128).T)
    bo_r = f(np.asarray(L0_attn_bo, np.float32).reshape(1, D))
    w1_r = f(w1.reshape(HC, 128, DC, 128).transpose(0, 3, 2, 1))
    w2_r = f(w2.reshape(HC, 128, DC, 128).transpose(0, 3, 2, 1))
    w3t = f(w3.T)

    cos_hi_a, cos_lo_a, sinm_hi_a, sinm_lo_a = _make_tables()

    nc = build_core_program()
    in_maps = []
    for c in range(NCORES):
        sl = slice(c * T, (c + 1) * T)
        in_maps.append({
            "x": f(x[sl]),
            "wq_r": wq_r, "wk_r": wk_r, "wv_r": wv_r, "wo_r": wo_r,
            "bq_c": bq_c, "bk_c": bk_c, "bv_c": bv_c, "bo_r": bo_r,
            "w1_r": w1_r, "w2_r": w2_r, "w3t": w3t,
            "cos_hi": f(cos_hi_a[:, sl]), "cos_lo": f(cos_lo_a[:, sl]),
            "sin_hi": f(sinm_hi_a[:, sl]), "sin_lo": f(sinm_lo_a[:, sl]),
        })
    return nc, in_maps


def kernel(**inputs):
    nc, in_maps = prepare_in_maps(**inputs)
    res = run_bass_kernel_spmd(nc, in_maps, core_ids=list(range(NCORES)))
    out = np.concatenate([res.results[c]["y"] for c in range(NCORES)], axis=0)
    return out.reshape(1, S, D)


if __name__ == "__main__":
    nc = build_core_program()
    print("built ok, instructions:", nc.n_instructions())
